# version 22
# baseline (speedup 1.0000x reference)
import contextlib
import numpy as np

import concourse.bass as bass
import concourse.mybir as mybir
from concourse.bass_utils import run_bass_kernel_spmd

F32 = mybir.dt.float32
BF16 = mybir.dt.bfloat16
U8 = mybir.dt.uint8
AF = mybir.ActivationFunctionType
ALU = mybir.AluOpType
AXX = mybir.AxisListType.X

D = 1024
DIN = 2048
NH = 32
CH = 128
NCONV = 2304
DIP = 4384
EPS = 1e-5
MAGIC = 12582912.0

NCORES = 8
SEQ_PER_CORE = 4
SEQLEN = 2048
TOKENS = SEQ_PER_CORE * SEQLEN
CHUNKS_PER_SEQ = SEQLEN // CH
NCC = 35
NCV = 18

ENGS = ("t", "v", "a", "g", "sy")


class Buf:
    __slots__ = ("w", "rs", "aliases")

    def __init__(self):
        self.w = None
        self.rs = {}
        self.aliases = []


class Plan:
    def __init__(self, sems):
        self.sems = sems
        self.items = {e: [] for e in ENGS}
        self.cnt = {name: 0 for name in sems}

    def _do(self, eng, fn, after, extra_incs=(), self_inc=True):
        sem = "s" + eng
        waits = {}
        for t in after:
            if t is None:
                continue
            s, v = t
            if s == sem and eng == "t":
                continue
            if v > waits.get(s, 0):
                waits[s] = v
        incs = []
        if self_inc:
            self.cnt[sem] += 1
            incs.append((sem, 1))
        for s2, amt in extra_incs:
            self.cnt[s2] += amt
            incs.append((s2, amt))
        self.items[eng].append((fn, waits, incs))
        if self_inc:
            return (sem, self.cnt[sem])
        return (extra_incs[0][0], self.cnt[extra_incs[0][0]])

    def op(self, eng, fn, r=(), w=(), after=(), extra_incs=(), self_inc=True):
        deps = list(after)
        for b in r:
            deps.append(b.w)
        for b in w:
            deps.append(b.w)
            deps.extend(b.rs.items())
            for ab in b.aliases:
                deps.append(ab.w)
                deps.extend(ab.rs.items())
        tok = self._do(eng, fn, deps, extra_incs, self_inc)
        for b in r:
            s, v = tok
            if v > b.rs.get(s, 0):
                b.rs[s] = v
        for b in w:
            b.w = tok
            b.rs = {}
        return tok

    def dma(self, eng, out, in_, sem, r=(), w=(), after=(), slow=False):
        def fn(E, out=out, in_=in_, slow=slow):
            return E.dma_start(out=out, in_=in_,
                               allow_slow_non_contiguous=slow)
        after = list(after) + [(sem, self.cnt[sem])]
        return self.op(eng, fn, r=r, w=w, after=after,
                       extra_incs=[(sem, 16)], self_inc=False)

    def emit(self, block):
        decos = {"t": block.tensor, "v": block.vector, "a": block.scalar,
                 "g": block.gpsimd, "sy": block.sync}
        for eng in ENGS:
            items = self.items[eng]
            if not items:
                continue

            def make(items):
                def run(E):
                    waited = {}
                    for fn, waits, incs in items:
                        for s, v in sorted(waits.items()):
                            if waited.get(s, 0) >= v:
                                continue
                            E.wait_ge(self.sems[s], v)
                            waited[s] = v
                        inst = fn(E)
                        for s, amt in incs:
                            inst = inst.then_inc(self.sems[s], amt)
                return run
            decos[eng](make(items))


def build_program():
    nchunks = TOKENS // CH
    nc = bass.Bass("TRN2", target_bir_lowering=False)

    xd = nc.dram_tensor("x", [TOKENS, D], F32, kind="ExternalInput").ap()
    d_nw = nc.dram_tensor("norm_w", [D], F32, kind="ExternalInput").ap()
    d_nb = nc.dram_tensor("norm_b", [D], F32, kind="ExternalInput").ap()
    d_wi = nc.dram_tensor("in_proj_w", [DIP, D], F32, kind="ExternalInput").ap()
    d_cw = nc.dram_tensor("conv_w", [NCONV, 4], F32, kind="ExternalInput").ap()
    d_cb = nc.dram_tensor("conv_b", [NCONV], F32, kind="ExternalInput").ap()
    d_dtb = nc.dram_tensor("dt_bias", [NH], F32, kind="ExternalInput").ap()
    d_alog = nc.dram_tensor("A_log", [NH], F32, kind="ExternalInput").ap()
    d_dd = nc.dram_tensor("Dp", [NH], F32, kind="ExternalInput").ap()
    d_rw = nc.dram_tensor("rms_w", [DIN], F32, kind="ExternalInput").ap()
    d_wo = nc.dram_tensor("out_proj_w", [D, DIN], F32, kind="ExternalInput").ap()
    d_wf = nc.dram_tensor("ffn_w", [D, D], mybir.dt.int8, kind="ExternalInput").ap()
    d_ws = nc.dram_tensor("ffn_scale", [D, 1], F32, kind="ExternalInput").ap()
    d_fb = nc.dram_tensor("ffn_bias", [D], F32, kind="ExternalInput").ap()
    outd = nc.dram_tensor("out", [TOKENS, D], F32, kind="ExternalOutput").ap()
    d_ccin = nc.dram_tensor("cc_in", [1, 8], F32)
    d_ccout = nc.dram_tensor("cc_out", [1, 8], F32)

    ctx = contextlib.ExitStack()
    S = {}

    def sbt(name, shape, dtype):
        S[name] = ctx.enter_context(nc.sbuf_tensor(name, shape, dtype))
        return S[name]

    for k in range(8):
        sbt(f"WIT{k}", [128, DIP], BF16)
    sbt("WOTA", [128, 32768], U8)
    sbt("CWDA", [128, 18432], U8)
    sbt("DDIAG", [128, 16 * 128], BF16)
    sbt("OHM", [64, NH * 128], BF16)
    sbt("IDENTF", [128, 128], F32)
    sbt("IDENTB", [128, 128], BF16)
    sbt("TRIL", [128, 128], BF16)
    sbt("ONESF", [128, 128], F32)
    sbt("BVEC", [128, NCC], F32)
    sbt("nwcol", [128, 8], F32)
    sbt("nbcolb", [128, 8], BF16)
    sbt("rwcol", [128, 16], F32)
    sbt("cwcol", [128, NCV * 4], F32)
    sbt("cbcol", [128, NCV], F32)
    sbt("dtbcol", [32, 1], F32)
    sbt("alogcol", [32, 1], F32)
    sbt("ACOL", [32, 1], F32)
    sbt("DROW", [1, 32], F32)
    sbt("DCEXP", [128, 16], F32)
    sbt("MU", [128, 64], F32)
    sbt("RSTD", [128, 64], F32)
    sbt("MAXC", [128, 1], F32)
    sbt("C127", [1, 8], F32)
    sbt("SX", [1, 8], F32)
    sbt("GMROW", [1, 8], F32)
    sbt("INVSC", [128, 1], F32)
    sbt("SCOLS", [128, 1], F32)
    sbt("ZER32", [32, CH], BF16)
    sbt("BN6", [128, 36], F32)
    sbt("BN2A", [128, 2], F32)
    sbt("BN2R", [128, 2], F32)
    sbt("LNC", [128, 4], F32)
    sbt("MXT", [128, 1], F32)
    sbt("XMN", [128, 1], F32)
    sbt("ECOL", [32, 2], F32)
    sbt("RCOL", [128, 2], F32)
    sbt("EPSC", [128, 1], F32)
    sbt("SCOL", [128, 14], F32)

    sbt("XT0", [128, 1024], F32)
    sbt("XT1", [128, 1024], F32)
    sbt("XN2T0", [128, 1024], BF16)
    sbt("XN2T1", [128, 1024], BF16)
    sbt("SZ0", [128, 2048], BF16)
    sbt("SZ1", [128, 2048], BF16)
    sbt("XBC0", [128, NCV * 128], BF16)
    sbt("XBC1", [128, NCV * 128], BF16)
    sbt("XPRE", [128, NCV * 132], BF16)
    sbt("DTF0", [32, 128], F32)
    sbt("DTF1", [32, 128], F32)
    sbt("AFM0", [32, 128], F32)
    sbt("AFM1", [32, 128], F32)
    sbt("ACUM", [32, 128], F32)
    sbt("ACUMR", [32, CH], F32)
    sbt("DDF", [32, CH], F32)
    sbt("EXPACF", [32, CH], F32)
    sbt("TMISC0", [128, 128], F32)
    sbt("TMISC1", [128, 128], F32)
    sbt("EXPTROW", [1, 32], F32)
    sbt("EXPTBC0", [128, 32], F32)
    sbt("EXPTBC1", [128, 32], F32)
    sbt("HILO0", [64, CH], BF16)
    sbt("HILO1", [64, CH], BF16)
    sbt("SM", [128, 128], F32)
    sbt("BTM", [128, 128], BF16)
    sbt("PREV", [128, DIN], F32)
    sbt("OUT0", [128, 1024], F32)
    sbt("OUT1", [128, 1024], F32)
    sbt("ARENA", [128, 17408], U8)

    ar = S["ARENA"].ap()
    arf = ar.bitcast(F32)
    arb = ar.bitcast(BF16)
    vY = arf[:, 0:2048]
    vQS = arf[:, 0:1024]
    vLBF = [arb[:, 5120:5632], arb[:, 4096:4608]]
    vMP = [arb[:, 5632:6144], arb[:, 4608:5120]]
    vXG = [arb[:, 6144:6656], arb[:, 6656:7168]]
    vXDG = arb[:, 7168:7680]
    vPVG = arb[:, 7680:8192]
    vSPS = arf[0:32, 3072:3328]
    vXN2N = arb[:, 7680:8704]
    vYRMS = arb[:, 4096:6144]
    vYFMT = arb[:, 6144:8192]
    vSTG = arb[:, 0:8192]
    vXQ = arb[:, 6144:7168]
    vXQT = arb[:, 7168:8192]
    vPT = [arf[:, 1024:2048], arf[:, 2048:3072]]

    wota = S["WOTA"].ap().bitcast(BF16)
    vWOT = wota[:, 0:16384]
    vWFT = wota[:, 0:8192]
    cwdb = S["CWDA"].ap().bitcast(BF16)
    cwdf = S["CWDA"].ap().bitcast(F32)
    vCWD = cwdb[:, 0:9216]
    vWSCOL = cwdf[:, 0:8]
    vFBROW = cwdf[0:1, 1024:2048]
    vFBIAS = cwdf[:, 3072:4096]

    PB = [ctx.enter_context(nc.psum_tensor(f"pb{i}", [128, 512], F32))
          for i in range(8)]
    PBb = [p.ap().bitcast(BF16) for p in PB]

    sems = {}
    for name in ("st", "sv", "sa", "sg", "ssy", "dx0", "dx1", "ds", "dw",
                 "do0", "do1", "dp0", "dp1", "scc", "gx0", "gx1"):
        sems[name] = ctx.enter_context(nc.semaphore(name))
    P = Plan(sems)

    B = {}
    for name in list(S.keys()):
        B[name] = Buf()
    for name in ("Y", "QS", "LBF0", "LBF1", "MP0", "MP1", "XG0", "XG1", "XDG",
                 "PVG", "XN2N", "SPS",
                 "YRMS", "YFMT", "STG", "XQ", "XQT", "PT0", "PT1",
                 "WOT", "WFT", "CWD", "WSCOL", "FBROW", "FBIAS"):
        B[name] = Buf()
    for i in range(8):
        B[f"PB{i}"] = Buf()
    for ccc in range(NCV):
        B[f"XP{ccc}"] = Buf()

    def alias(a, b):
        B[a].aliases.append(B[b])
        B[b].aliases.append(B[a])

    for nm in ("Y", "QS", "LBF0", "LBF1", "MP0", "MP1", "XG0", "XG1", "XDG",
               "PVG", "SPS", "YRMS", "YFMT", "XQ", "XQT", "PT0", "PT1", "XN2N"):
        alias("STG", nm)
    alias("QS", "Y")
    for nm in ("LBF0", "LBF1", "MP0", "MP1", "PT1"):
        alias("YRMS", nm)
    for nm in ("XG0", "XG1", "XDG", "PVG", "SPS", "XQ", "XQT", "XN2N"):
        alias("YFMT", nm)
    alias("XQ", "XG0")
    alias("XQ", "XG1")
    alias("XQ", "SPS")
    alias("XQT", "XDG")
    alias("XQT", "PVG")
    alias("XQT", "XN2N")
    alias("SPS", "XG0")
    alias("XN2N", "PVG")
    alias("PT0", "Y")
    for nm in ("LBF0", "LBF1", "MP0", "MP1"):
        alias("PT1", nm)
    alias("WOT", "WFT")
    for nm in ("WSCOL", "FBROW", "FBIAS"):
        alias("CWD", nm)

    XT = [S["XT0"], S["XT1"]]
    OUT = [S["OUT0"], S["OUT1"]]
    XN2T = [S["XN2T0"], S["XN2T1"]]
    SZ = [S["SZ0"], S["SZ1"]]
    XBC = [S["XBC0"], S["XBC1"]]
    DTF = [S["DTF0"], S["DTF1"]]
    AFM = [S["AFM0"], S["AFM1"]]

    def rb(*names):
        return [B[n] for n in names]

    P.op("g", lambda g: g.memset(S["ONESF"][:, :], 1.0), w=rb("ONESF"))
    P.op("g", lambda g: g.affine_select(
        S["IDENTF"][:, :], S["ONESF"][:, :], [[1, 128]], ALU.is_equal, 0.0,
        channel_multiplier=-1), r=rb("ONESF"), w=rb("IDENTF"))
    P.op("v", lambda v: v.tensor_copy(S["IDENTB"][:, :], S["IDENTF"][:, :]),
         r=rb("IDENTF"), w=rb("IDENTB"))
    P.op("g", lambda g: g.memset(S["TRIL"][:, :], 1.0), w=rb("TRIL"))
    P.op("g", lambda g: g.affine_select(
        S["TRIL"][:, :], S["TRIL"][:, :], [[1, 128]], ALU.is_ge, 0.0,
        channel_multiplier=-1), r=rb("TRIL"), w=rb("TRIL"))
    P.op("g", lambda g: g.memset(S["OHM"][:, :], 1.0), w=rb("OHM"))
    P.op("g", lambda g: g.affine_select(
        S["OHM"][0:32, :], S["OHM"][0:32, :], [[1, 32], [0, 128]],
        ALU.is_equal, 0.0, channel_multiplier=-1), r=rb("OHM"), w=rb("OHM"))
    P.op("g", lambda g: g.affine_select(
        S["OHM"][32:64, :], S["OHM"][32:64, :], [[1, 32], [0, 128]],
        ALU.is_equal, 0.0, channel_multiplier=-1), r=rb("OHM"), w=rb("OHM"))
    P.op("g", lambda g: g.memset(S["ZER32"][:, :], 0.0), w=rb("ZER32"))
    P.op("g", lambda g: g.memset(S["MAXC"][:, :], 0.0), w=rb("MAXC"))
    P.op("g", lambda g: g.memset(S["C127"][:, :], 127.0), w=rb("C127"))
    P.op("g", lambda g: g.memset(S["SX"][:, :], 0.0), w=rb("SX"))
    P.op("g", lambda g: g.memset(S["EPSC"][:, :], EPS), w=rb("EPSC"))

    P.dma("sy", S["nwcol"][:, :], d_nw.rearrange("(a p) -> p a", p=128), "ds",
          w=rb("nwcol"), slow=True)
    P.dma("sy", S["rwcol"][:, :], d_rw.rearrange("(a p) -> p a", p=128), "ds",
          w=rb("rwcol"), slow=True)
    P.dma("sy", S["cwcol"][:, :].rearrange("p (cc t) -> p cc t", cc=NCV),
          d_cw.rearrange("(cc p) t -> p cc t", p=128), "ds", w=rb("cwcol"),
          slow=True)
    P.dma("sy", S["cbcol"][:, :], d_cb.rearrange("(cc p) -> p cc", p=128), "ds",
          w=rb("cbcol"), slow=True)
    P.dma("sy", S["dtbcol"][:, :], d_dtb.unsqueeze(1), "ds",
          w=rb("dtbcol"), slow=True)
    P.dma("sy", S["alogcol"][:, :], d_alog.unsqueeze(1), "ds",
          w=rb("alogcol"), slow=True)
    P.dma("sy", S["DROW"][0:1, :], d_dd.unsqueeze(0), "ds", w=rb("DROW"))
    P.dma("g", S["nbcolb"][:, :], d_nb.rearrange("(a p) -> p a", p=128), "dw",
          w=rb("nbcolb"), slow=True)

    P.op("a", lambda a: a.activation(S["ACOL"][:, :], S["alogcol"][:, :], AF.Exp),
         r=rb("alogcol"), w=rb("ACOL"))
    P.op("v", lambda v: v.tensor_scalar_mul(S["ACOL"][:, :], S["ACOL"][:, :], -1.0),
         r=rb("ACOL"), w=rb("ACOL"))

    P.op("t", lambda t: t.matmul(PB[7][:, 0:32], S["ONESF"][0:1, :],
                                 S["DROW"][0:1, :], start=True, stop=True),
         r=rb("ONESF", "DROW"), w=rb("PB7"))
    P.op("v", lambda v: v.tensor_copy(S["DCEXP"][0:64, :], PB[7][0:64, 0:32:2]),
         r=rb("PB7"), w=rb("DCEXP"))
    P.op("v", lambda v: v.tensor_copy(S["DCEXP"][64:128, :], PB[7][64:128, 1:32:2]),
         r=rb("PB7"), w=rb("DCEXP"))
    for gc in range(16):
        P.op("v", lambda v, gc=gc: v.tensor_scalar_mul(
            S["DDIAG"][:, gc * 128:(gc + 1) * 128], S["IDENTB"][:, :],
            S["DCEXP"][:, gc:gc + 1]), r=rb("IDENTB", "DCEXP"), w=rb("DDIAG"))
    for cc in range(NCV):
        for tap in range(4):
            P.op("v", lambda v, cc=cc, tap=tap: v.tensor_scalar_mul(
                vCWD[:, (cc * 4 + tap) * 128:(cc * 4 + tap + 1) * 128],
                S["IDENTB"][:, :], S["cwcol"][:, cc * 4 + tap:cc * 4 + tap + 1]),
                r=rb("IDENTB", "cwcol"), w=rb("CWD"))

    wstg = wota[:, 0:8192]
    piece_ccs = [list(range(0, 8)), list(range(8, 16)),
                 list(range(16, 24)), list(range(24, 32)), list(range(32, 34))]

    def piece_dma(i):
        ccs = piece_ccs[i]
        r0, nr, na = ccs[0] * 128, len(ccs) * 128, len(ccs)
        stg = vSTG if i % 2 == 0 else wstg
        bn = "STG" if i % 2 == 0 else "WOT"
        P.dma("g", stg[:, 0:na * 1024].rearrange("p (a k) -> p a k", a=na),
              d_wi[r0:r0 + nr, :].rearrange("(a p) k -> p a k", p=128), "dw",
              w=rb(bn))

    piece_dma(0)
    piece_dma(1)
    for i, ccs in enumerate(piece_ccs):
        na = len(ccs)
        stg = vSTG if i % 2 == 0 else wstg
        bn = "STG" if i % 2 == 0 else "WOT"
        for k in range(8):
            for gi in range(0, na, 4):
                gccs = ccs[gi:gi + 4]
                n = len(gccs)
                for j, cc in enumerate(gccs):
                    a = cc - ccs[0]
                    P.op("t", lambda t, a=a, j=j, k=k, stg=stg: t.transpose(
                        PBb[5][:, j * 128:(j + 1) * 128],
                        stg[:, a * 1024 + k * 128:a * 1024 + (k + 1) * 128],
                        S["IDENTB"][:, :]),
                        r=rb(bn, "IDENTB"), w=rb("PB5"))
                P.op("v", lambda v, k=k, c0=gccs[0], n=n: v.tensor_copy(
                    S[f"WIT{k}"][:, c0 * 128:(c0 + n) * 128], PBb[5][:, 0:n * 128]),
                    r=rb("PB5"), w=rb(f"WIT{k}"))
        if i + 2 < len(piece_ccs):
            piece_dma(i + 2)
    P.dma("g", wstg[0:32, 0:1024], d_wi[4352:4384, :], "dw", w=rb("WOT"))
    for k in range(8):
        P.op("t", lambda t, k=k: t.transpose(
            PBb[5][:, 0:32], wstg[0:32, k * 128:(k + 1) * 128],
            S["IDENTB"][0:32, 0:32]), r=rb("WOT", "IDENTB"), w=rb("PB5"))
        P.op("v", lambda v, k=k: v.tensor_copy(
            S[f"WIT{k}"][:, 4352:4384], PBb[5][:, 0:32]),
            r=rb("PB5"), w=rb(f"WIT{k}"))
    P.dma("g", vSTG[:, 0:4 * 2048].rearrange("p (a f) -> p a f", a=4),
          d_wo[0:4 * 128, :].rearrange("(a p) f -> p a f", p=128), "dw",
          w=rb("STG"))
    for cc in range(NCC):
        w = 128 if cc < 34 else 32
        for k in range(8):
            P.op("t", lambda t, k=k, cc=cc, w=w: t.matmul(
                PB[7][0:w, 0:1],
                S[f"WIT{k}"][:, cc * 128:cc * 128 + w],
                S["nbcolb"][:, k:k + 1], start=(k == 0), stop=(k == 7)),
                r=rb(f"WIT{k}", "nbcolb"), w=rb("PB7"))
        P.op("v", lambda v, cc=cc, w=w: v.tensor_copy(
            S["BVEC"][0:w, cc:cc + 1], PB[7][0:w, 0:1]),
            r=rb("PB7"), w=rb("BVEC"))
    P.op("v", lambda v: v.tensor_add(S["dtbcol"][:, :], S["dtbcol"][:, :],
                                     S["BVEC"][0:32, 34:35]),
         r=rb("dtbcol", "BVEC"), w=rb("dtbcol"))
    for k in range(8):
        P.op("v", lambda v, k=k: v.tensor_scalar_mul(
            S[f"WIT{k}"][:, :], S[f"WIT{k}"][:, :], S["nwcol"][:, k:k + 1]),
            r=rb("nwcol", f"WIT{k}"), w=rb(f"WIT{k}"))

    for piece in range(2):
        a0 = piece * 4
        if piece > 0:
            P.dma("g", vSTG[:, 0:4 * 2048].rearrange("p (a f) -> p a f", a=4),
                  d_wo[a0 * 128:(a0 + 4) * 128, :].rearrange(
                      "(a p) f -> p a f", p=128), "dw", w=rb("STG"))
        for fc in range(16):
            for j in range(4):
                P.op("t", lambda t, j=j, fc=fc: t.transpose(
                    PBb[6][:, j * 128:(j + 1) * 128],
                    vSTG[:, j * 2048 + fc * 128:j * 2048 + (fc + 1) * 128],
                    S["IDENTB"][:, :]), r=rb("STG", "IDENTB"), w=rb("PB6"))
            P.op("v", lambda v, fc=fc, a0=a0: v.tensor_scalar_mul(
                vWOT[:, fc * 1024 + a0 * 128:fc * 1024 + (a0 + 4) * 128],
                PBb[6][:, 0:512], S["rwcol"][:, fc:fc + 1]),
                r=rb("PB6", "rwcol"), w=rb("WOT"))

    def xin(cg):
        return xd[cg * 128:(cg + 1) * 128, :]

    t_store = [None] * nchunks
    P.dma("sy", XT[0][:, :], xin(0), "dx0", w=rb("XT0"))
    P.dma("sy", XT[1][:, :], xin(1), "dx1", w=rb("XT1"))

    def emit_A1(c):
        p = c % 2
        xt, xb = XT[p], B[f"XT{p}"]
        P.op("v", lambda v, xt=xt: v.bn_stats(S["BN6"][:, 24:30], xt[:, 0:512]),
             r=[xb], w=rb("BN6"))
        P.op("v", lambda v, xt=xt: v.bn_stats(S["BN6"][:, 30:36], xt[:, 512:1024]),
             r=[xb], w=rb("BN6"))
        P.op("v", lambda v: v.bn_aggr(S["BN2A"][:, :], S["BN6"][:, 24:36]),
             r=rb("BN6"), w=rb("BN2A"))
        P.op("v", lambda v, c=c: v.tensor_copy(S["MU"][:, c:c + 1],
                                               S["BN2A"][:, 0:1]),
             r=rb("BN2A"), w=rb("MU"))
        P.op("a", lambda a: a.activation(S["LNC"][:, 0:1], S["BN2A"][:, 1:2],
                                         AF.Ln, bias=S["EPSC"][:, 0:1]),
             r=rb("BN2A", "EPSC"), w=rb("LNC"))
        P.op("a", lambda a, c=c: a.activation(
            S["RSTD"][:, c:c + 1], S["LNC"][:, 0:1], AF.Exp, scale=-0.5),
            r=rb("LNC"), w=rb("RSTD"))
        P.op("v", lambda v, c=c: v.tensor_scalar_mul(
            S["LNC"][:, 2:3], S["BN2A"][:, 1:2], S["RSTD"][:, c:c + 1]),
            r=rb("BN2A", "RSTD"), w=rb("LNC"))
        P.op("v", lambda v, c=c: v.tensor_scalar_mul(
            S["LNC"][:, 2:3], S["LNC"][:, 2:3], S["RSTD"][:, c:c + 1]),
            r=rb("RSTD", "LNC"), w=rb("LNC"))
        P.op("a", lambda a: a.activation(S["LNC"][:, 3:4], S["LNC"][:, 2:3],
                                         AF.Ln, bias=S["EPSC"][:, 0:1]),
             r=rb("LNC", "EPSC"), w=rb("LNC"))
        P.op("a", lambda a, p=p: a.activation(
            S["SCOL"][:, 0 + p:1 + p], S["LNC"][:, 3:4], AF.Exp, scale=-0.5),
            r=rb("LNC"), w=rb("SCOL"))
        P.op("a", lambda a, p=p: a.activation(
            S["SCOL"][:, 2 + p:3 + p], S["LNC"][:, 3:4], AF.Exp, scale=0.5),
            r=rb("LNC"), w=rb("SCOL"))
        P.op("v", lambda v, c=c, p=p: v.tensor_scalar_mul(
            S["SCOL"][:, 4 + p:5 + p], S["SCOL"][:, 0 + p:1 + p],
            S["RSTD"][:, c:c + 1]), r=rb("RSTD", "SCOL"), w=rb("SCOL"))
        P.op("v", lambda v, c=c, p=p: v.tensor_scalar_add(
            S["SCOL"][:, 6 + p:7 + p], S["RSTD"][:, c:c + 1], 1.0),
            r=rb("RSTD"), w=rb("SCOL"))
        P.op("v", lambda v, c=c, p=p: v.tensor_scalar(
            S["SCOL"][:, 8 + p:9 + p], S["MU"][:, c:c + 1],
            S["RSTD"][:, c:c + 1], -1.0, ALU.mult, ALU.mult),
            r=rb("MU", "RSTD"), w=rb("SCOL"))
        P.op("v", lambda v, c=c, p=p: v.tensor_scalar(
            S["SCOL"][:, 12 + p:13 + p], S["MU"][:, c:c + 1],
            S["SCOL"][:, 4 + p:5 + p], -1.0, ALU.mult, ALU.mult),
            r=rb("MU", "SCOL"), w=rb("SCOL"))
        P.op("v", lambda v, xt=xt: v.reduce_max(S["MXT"][:, :], xt[:, :],
                                                axis=AXX),
             r=[xb], w=rb("MXT"))
        P.op("v", lambda v, xt=xt: v.tensor_reduce(S["XMN"][:, :], xt[:, :],
                                                   op=ALU.min, axis=AXX),
             r=[xb], w=rb("XMN"))
        P.op("v", lambda v, c=c: v.tensor_scalar_sub(
            S["MXT"][:, :], S["MXT"][:, :], S["MU"][:, c:c + 1]),
            r=rb("MU", "MXT"), w=rb("MXT"))
        P.op("v", lambda v, c=c: v.tensor_scalar(
            S["XMN"][:, :], S["XMN"][:, :], S["MU"][:, c:c + 1], -1.0,
            ALU.subtract, ALU.mult), r=rb("MU", "XMN"), w=rb("XMN"))
        P.op("v", lambda v: v.tensor_max(S["MXT"][:, :], S["MXT"][:, :],
                                         S["XMN"][:, :]),
             r=rb("XMN", "MXT"), w=rb("MXT"))
        P.op("v", lambda v, c=c: v.tensor_scalar_mul(
            S["MXT"][:, :], S["MXT"][:, :], S["RSTD"][:, c:c + 1]),
            r=rb("RSTD", "MXT"), w=rb("MXT"))
        P.op("v", lambda v: v.tensor_max(S["MAXC"][:, :], S["MAXC"][:, :],
                                         S["MXT"][:, :]),
             r=rb("MXT", "MAXC"), w=rb("MAXC"))

    def emit_A2v(c):
        p = c % 2
        xt, xb = XT[p], B[f"XT{p}"]
        P.op("a", lambda a, xt=xt, p=p: a.activation(
            vXN2N, xt[:, :], AF.Identity,
            bias=S["SCOL"][:, 12 + p:13 + p], scale=S["SCOL"][:, 4 + p:5 + p]),
            r=[xb] + rb("SCOL"), w=rb("XN2N"))

    def emit_A2t(c):
        p = c % 2
        for gk in range(2):
            for j in range(4):
                k = gk * 4 + j
                P.op("t", lambda t, k=k, j=j: t.transpose(
                    PBb[3][:, j * 128:(j + 1) * 128],
                    vXN2N[:, k * 128:(k + 1) * 128], S["IDENTB"][:, :]),
                    r=rb("XN2N", "IDENTB"), w=rb("PB3"))
            P.op("a", lambda a, gk=gk, p=p: a.copy(
                XN2T[p][:, gk * 512:(gk + 1) * 512], PBb[3][:, 0:512]),
                r=rb("PB3"), w=[B[f"XN2T{p}"]])

    def emit_B1(c):
        p = c % 2
        seq_start = (c % CHUNKS_PER_SEQ == 0)
        for cc in range(NCC):
            w = 128 if cc < 34 else 32
            bank = cc % 3
            for k in range(8):
                P.op("t", lambda t, k=k, cc=cc, w=w, bank=bank, p=p: t.matmul(
                    PB[bank][0:w, 0:128], S[f"WIT{k}"][:, cc * 128:cc * 128 + w],
                    XN2T[p][:, k * 128:(k + 1) * 128],
                    start=(k == 0), stop=(k == 7)),
                    r=[B[f"WIT{k}"], B[f"XN2T{p}"]], w=[B[f"PB{bank}"]])
            if cc < 16:
                P.op("a", lambda a, cc=cc, bank=bank, p=p: a.activation(
                    SZ[p][:, cc * 128:(cc + 1) * 128], PB[bank][:, 0:128],
                    AF.Silu, bias=S["BVEC"][:, cc:cc + 1]),
                    r=[B[f"PB{bank}"], B["BVEC"]], w=[B[f"SZ{p}"]])
            elif cc < 34:
                ccc = cc - 16
                base = ccc * 132
                xpb = B[f"XP{ccc}"]
                if not seq_start:
                    P.op("v", lambda v, base=base: v.tensor_copy(
                        S["XPRE"][:, base:base + 3],
                        S["XPRE"][:, base + 128:base + 131]), w=[xpb])
                P.op("a", lambda a, base=base, cc=cc, bank=bank: a.activation(
                    S["XPRE"][:, base + 3:base + 131], PB[bank][:, 0:128],
                    AF.Identity, bias=S["BVEC"][:, cc:cc + 1]),
                    r=[B[f"PB{bank}"], B["BVEC"]], w=[xpb])

    def emit_Bconv(c):
        p = c % 2
        for ccc in range(NCV):
            base = ccc * 132
            bank = [3, 0, 2][ccc % 3]
            pbuf = B[f"PB{bank}"]
            xpb = B[f"XP{ccc}"]
            for tap in range(4):
                P.op("t", lambda t, ccc=ccc, tap=tap, base=base, bank=bank: t.matmul(
                    PB[bank][:, 0:128],
                    vCWD[:, (ccc * 4 + tap) * 128:(ccc * 4 + tap + 1) * 128],
                    S["XPRE"][:, base + tap:base + tap + 128],
                    start=(tap == 0), stop=(tap == 3)),
                    r=[xpb, B["CWD"]], w=[pbuf])
            P.op("a", lambda a, ccc=ccc, p=p, bank=bank: a.activation(
                XBC[p][:, ccc * 128:(ccc + 1) * 128], PB[bank][:, 0:128],
                AF.Silu, bias=S["cbcol"][:, ccc:ccc + 1]),
                r=[pbuf, B["cbcol"]], w=[B[f"XBC{p}"]])

    def emit_dt(c):
        p = c % 2
        sp = vSPS[:, 0:128]
        P.op("v", lambda v, p=p: v.tensor_scalar_add(
            DTF[p][:, :], PB[1][0:32, 0:128], S["dtbcol"][:, :]),
            r=[B["PB1"], B["dtbcol"]], w=[B[f"DTF{p}"]])
        P.op("v", lambda v, p=p: v.tensor_scalar_min(
            AFM[p][:, :], DTF[p][:, :], 30.0),
            r=[B[f"DTF{p}"]], w=[B[f"AFM{p}"]])
        P.op("a", lambda a, sp=sp, p=p: a.activation(
            sp, AFM[p][:, :], AF.Exp), r=[B[f"AFM{p}"]], w=rb("SPS"))
        P.op("v", lambda v, sp=sp: v.tensor_scalar_add(sp, sp, 1.0),
             r=rb("SPS"), w=rb("SPS"))
        P.op("a", lambda a, sp=sp, p=p: a.activation(
            AFM[p][:, :], sp, AF.Ln), r=rb("SPS"), w=[B[f"AFM{p}"]])
        P.op("v", lambda v, p=p: v.tensor_max(
            DTF[p][:, :], AFM[p][:, :], DTF[p][:, :]),
            r=[B[f"AFM{p}"], B[f"DTF{p}"]], w=[B[f"DTF{p}"]])
        P.op("v", lambda v, p=p: v.tensor_scalar_mul(
            AFM[p][:, :], DTF[p][:, :], S["ACOL"][:, :]),
            r=[B[f"DTF{p}"], B["ACOL"]], w=[B[f"AFM{p}"]])

    def emit_Cprep(c):
        p = c % 2
        P.op("v", lambda v, p=p: v.tensor_tensor_scan(
            S["ACUM"][:, :], AFM[p][:, :], S["ZER32"][:, :], 0.0,
            ALU.add, ALU.add), r=[B[f"AFM{p}"], B["ZER32"]], w=rb("ACUM"))
        lastc = S["ACUM"][:, 127:128]
        P.op("v", lambda v, lastc=lastc: v.tensor_scalar(
            S["DDF"][:, :], S["ACUM"][:, :], lastc, -1.0,
            ALU.subtract, ALU.mult), r=rb("ACUM"), w=rb("DDF"))
        P.op("a", lambda a: a.activation(S["EXPACF"][:, :], S["DDF"][:, :],
                                         AF.Exp), r=rb("DDF"), w=rb("EXPACF"))
        P.op("v", lambda v, p=p: v.tensor_mul(
            S["DDF"][:, :], DTF[p][:, :], S["EXPACF"][:, :]),
            r=[B[f"DTF{p}"], B["EXPACF"]], w=rb("DDF"))
        P.op("a", lambda a, lastc=lastc: a.activation(
            S["ECOL"][:, 0:1], lastc, AF.Exp), r=rb("ACUM"), w=rb("ECOL"))
        P.op("v", lambda v, lastc=lastc: v.tensor_scalar_mul(
            S["ECOL"][:, 1:2], lastc, 0.5), r=rb("ACUM"), w=rb("ECOL"))
        P.op("v", lambda v: v.tensor_scalar_sub(
            S["ACUMR"][:, :], S["ACUM"][:, :], S["ECOL"][:, 1:2]),
            r=rb("ACUM", "ECOL"), w=rb("ACUMR"))
        P.op("a", lambda a: a.activation(S["EXPACF"][:, :], S["ACUM"][:, :],
                                         AF.Exp), r=rb("ACUM"), w=rb("EXPACF"))
        srcs = [(S["ACUMR"][:, :], "ACUMR"), (DTF[p][:, :], f"DTF{p}"),
                (S["DDF"][:, :], "DDF"), (S["EXPACF"][:, :], "EXPACF")]
        for j, (src, nm) in enumerate(srcs):
            P.op("t", lambda t, j=j, src=src: t.transpose(
                PB[5][:, j * 32:(j + 1) * 32], src, S["IDENTF"][0:32, 0:32]),
                r=[B[nm], B["IDENTF"]], w=rb("PB5"))
        P.op("a", lambda a, p=p: a.copy(S[f"TMISC{p}"][:, :], PB[5][:, 0:128]),
             r=rb("PB5"), w=[B[f"TMISC{p}"]])
        P.op("t", lambda t: t.transpose(PB[5][0:1, 128:160], S["ECOL"][:, 0:1],
                                        S["IDENTF"][0:32, 0:32]),
             r=rb("ECOL", "IDENTF"), w=rb("PB5"))
        P.op("a", lambda a: a.copy(S["EXPTROW"][0:1, :], PB[5][0:1, 128:160]),
             r=rb("PB5"), w=rb("EXPTROW"))
        P.op("t", lambda t: t.matmul(PB[5][:, 160:192], S["ONESF"][0:1, :],
                                     S["EXPTROW"][0:1, :], start=True, stop=True),
             r=rb("ONESF", "EXPTROW"), w=rb("PB5"))
        P.op("a", lambda a, p=p: a.copy(S[f"EXPTBC{p}"][:, :], PB[5][:, 160:192]),
             r=rb("PB5"), w=[B[f"EXPTBC{p}"]])
        P.op("v", lambda v, p=p: v.tensor_copy(
            S[f"HILO{p}"][0:32, :], S["ACUMR"][:, :]),
            r=rb("ACUMR"), w=[B[f"HILO{p}"]])
        P.op("v", lambda v, p=p: v.tensor_sub(
            S[f"HILO{p}"][32:64, :], S["ACUMR"][:, :], S[f"HILO{p}"][0:32, :]),
            r=rb("ACUMR"), w=[B[f"HILO{p}"]])

    def emit_Cmain(c):
        p = c % 2
        xbc, xbcb = XBC[p], B[f"XBC{p}"]
        TM, TMb = S[f"TMISC{p}"], B[f"TMISC{p}"]
        HL, HLb = S[f"HILO{p}"], B[f"HILO{p}"]
        ETB, ETBb = S[f"EXPTBC{p}"], B[f"EXPTBC{p}"]
        boff = 16 * 128
        coff = 17 * 128
        P.op("t", lambda t, xbc=xbc: t.matmul(
            PB[6][:, 0:128], xbc[:, boff:boff + 128],
            xbc[:, coff:coff + 128], start=True, stop=True),
            r=[xbcb], w=rb("PB6"))
        P.op("v", lambda v: v.tensor_mul(S["SM"][:, :], PB[6][:, 0:128],
                                         S["TRIL"][:, :]),
             r=rb("PB6", "TRIL"), w=rb("SM"))
        P.op("t", lambda t, xbc=xbc: t.transpose(
            PBb[6][:, 256:384], xbc[:, boff:boff + 128], S["IDENTB"][:, :]),
            r=[xbcb, B["IDENTB"]], w=rb("PB6"))
        P.op("a", lambda a: a.copy(S["BTM"][:, :], PBb[6][:, 256:384]),
             r=rb("PB6"), w=rb("BTM"))

        xstb = ["PB5", "PB1"]
        yab = ["PB7", "PB0"]
        yob = ["PB6", "PB2"]
        stb = ["PB4", "PB3"]
        ohb = ["PB6", "PB2"]

        def xst(g):
            bank = [5, 1][g % 2]
            for i in range(4):
                xoff = (4 * g + i) * 128
                P.op("t", lambda t, xoff=xoff, i=i, xbc=xbc, bank=bank: t.transpose(
                    PBb[bank][:, i * 128:(i + 1) * 128],
                    xbc[:, xoff:xoff + 128], S["IDENTB"][:, :]),
                    r=[xbcb, B["IDENTB"]], w=[B[xstb[g % 2]]])

        def xgv(g):
            bank = [5, 1][g % 2]
            dt_b = TM[:, 32 + 8 * g:40 + 8 * g].unsqueeze(2).broadcast_to(
                [128, 8, 64])
            dd_b = TM[:, 64 + 8 * g:72 + 8 * g].unsqueeze(2).broadcast_to(
                [128, 8, 64])
            P.op("v", lambda v, dt_b=dt_b, bank=bank, g=g: v.tensor_tensor(
                vXG[g % 2].rearrange("p (h f) -> p h f", h=8),
                PBb[bank][:, 0:512].rearrange("p (h f) -> p h f", h=8),
                dt_b, ALU.mult),
                r=[B[xstb[g % 2]], TMb], w=[B[f"XG{g % 2}"]])
            P.op("v", lambda v, dd_b=dd_b, bank=bank: v.tensor_tensor(
                vXDG.rearrange("p (h f) -> p h f", h=8),
                PBb[bank][:, 0:512].rearrange("p (h f) -> p h f", h=8),
                dd_b, ALU.mult),
                r=[B[xstb[g % 2]], TMb], w=rb("XDG"))

        def states(g):
            bank = [4, 3][g % 2]
            P.op("t", lambda t, bank=bank: t.matmul(
                PB[bank][:, 0:512], S["BTM"][:, :], vXDG,
                start=True, stop=True),
                r=rb("BTM", "XDG"), w=[B[stb[g % 2]]])

        def pvg(g):
            P.op("a", lambda a, g=g: a.copy(
                vPVG, S["PREV"][:, g * 512:(g + 1) * 512]),
                r=rb("PREV"), w=[B["PVG"]])

        def ohm(g, sub):
            ob = [6, 2][sub]
            for j in range(4):
                h = 8 * g + 4 * sub + j
                P.op("t", lambda t, h=h, j=j, ob=ob: t.matmul(
                    PB[ob][:, j * 128:(j + 1) * 128],
                    S["OHM"][:, h * 128:(h + 1) * 128], HL[:, :],
                    start=True, stop=True),
                    r=[B["OHM"], HLb], w=[B[ohb[sub]]])

        def chain(g, sub):
            ob = [6, 2][sub]
            sp = sub
            ac_b = TM[:, 8 * g + 4 * sub:8 * g + 4 * sub + 4] \
                .unsqueeze(2).broadcast_to([128, 4, 128])
            P.op("v", lambda v, ac_b=ac_b, ob=ob: v.tensor_tensor(
                PB[ob][:, 0:512].rearrange("p (h f) -> p h f", h=4),
                PB[ob][:, 0:512].rearrange("p (h f) -> p h f", h=4),
                ac_b, ALU.subtract),
                r=[TMb], w=[B[ohb[sub]]])
            P.op("v", lambda v, ob=ob: v.tensor_scalar_min(
                PB[ob][:, 0:512], PB[ob][:, 0:512], 0.0),
                w=[B[ohb[sub]]])
            P.op("a", lambda a, ob=ob, sp=sp: a.activation(
                vLBF[sp], PB[ob][:, 0:512], AF.Exp),
                r=[B[ohb[sub]]], w=[B[f"LBF{sp}"]])
            sm_b = S["SM"][:, :].unsqueeze(1).broadcast_to([128, 4, 128])
            P.op("g", lambda g_, sm_b=sm_b, sp=sp: g_.tensor_tensor(
                vMP[sp].rearrange("p (h f) -> p h f", h=4),
                vLBF[sp].rearrange("p (h f) -> p h f", h=4), sm_b, ALU.mult),
                r=[B[f"LBF{sp}"], B["SM"]], w=[B[f"MP{sp}"]])

        def ydiag(g, sub):
            sp = sub
            for j in range(4):
                i = 4 * sub + j
                P.op("t", lambda t, i=i, j=j, g=g, sp=sp: t.matmul(
                    PB[[7, 0][g % 2]][:, i * 64:(i + 1) * 64],
                    vMP[sp][:, j * 128:(j + 1) * 128],
                    vXG[g % 2][:, i * 64:(i + 1) * 64],
                    start=(i == 0), stop=False),
                    r=[B[f"MP{sp}"], B[f"XG{g % 2}"]], w=[B[yab[g % 2]]])

        xst(0)
        xgv(0)
        states(0)
        pvg(0)
        for g in range(4):
            P.op("t", lambda t, xbc=xbc, g=g: t.matmul(
                PB[[6, 2][g % 2]][:, 0:512], xbc[:, coff:coff + 128],
                vPVG, start=True, stop=True),
                r=[xbcb, B["PVG"]], w=[B[yob[g % 2]]])
            P.op("v", lambda v, g=g: v.tensor_add(
                S["PREV"][:, g * 512:(g + 1) * 512],
                S["PREV"][:, g * 512:(g + 1) * 512], PB[[4, 3][g % 2]][:, 0:512]),
                r=[B[stb[g % 2]], B["PREV"]], w=rb("PREV"))
            et_b = ETB[:, 8 * g:8 * (g + 1)].unsqueeze(2).broadcast_to(
                [128, 8, 64])
            P.op("g", lambda g_, g=g, et_b=et_b: g_.tensor_tensor(
                S["PREV"][:, g * 512:(g + 1) * 512].rearrange(
                    "p (h f) -> p h f", h=8),
                S["PREV"][:, g * 512:(g + 1) * 512].rearrange(
                    "p (h f) -> p h f", h=8), et_b, ALU.mult),
                r=[ETBb, B["PREV"]], w=rb("PREV"))
            ea_b = TM[:, 96 + 8 * g:104 + 8 * g].unsqueeze(2).broadcast_to(
                [128, 8, 64])
            P.op("v", lambda v, g=g, ea_b=ea_b: v.tensor_tensor(
                vY[:, g * 512:(g + 1) * 512].rearrange("p (h f) -> p h f", h=8),
                PB[[6, 2][g % 2]][:, 0:512].rearrange("p (h f) -> p h f", h=8),
                ea_b, ALU.mult), r=[B[yob[g % 2]], TMb], w=rb("Y"))
            ohm(g, 0)
            ohm(g, 1)
            if g + 1 < 4:
                xst(g + 1)
            chain(g, 0)
            chain(g, 1)
            if g + 1 < 4:
                xgv(g + 1)
                states(g + 1)
                pvg(g + 1)
            ydiag(g, 0)
            ydiag(g, 1)
            for i2 in range(4):
                fc = 4 * g + i2
                xoff = fc * 128
                P.op("t", lambda t, i2=i2, fc=fc, xoff=xoff, xbc=xbc, g=g: t.matmul(
                    PB[[7, 0][g % 2]][:, i2 * 128:(i2 + 1) * 128],
                    xbc[:, xoff:xoff + 128],
                    S["DDIAG"][:, fc * 128:(fc + 1) * 128],
                    start=False, stop=(i2 == 3)),
                    r=[xbcb, B["DDIAG"]], w=[B[yab[g % 2]]])
            P.op("v", lambda v, g=g: v.tensor_add(
                vY[:, g * 512:(g + 1) * 512], vY[:, g * 512:(g + 1) * 512],
                PB[[7, 0][g % 2]][:, 0:512]),
                r=[B[yab[g % 2]], B["Y"]], w=rb("Y"))

    def emit_tail(c):
        p = c % 2

        def zt(q):
            bank = [5, 1][q % 2]
            for i in range(4):
                zoff = (4 * q + i) * 128
                P.op("t", lambda t, i=i, zoff=zoff, p=p, bank=bank: t.transpose(
                    PBb[bank][:, i * 128:(i + 1) * 128],
                    SZ[p][:, zoff:zoff + 128], S["IDENTB"][:, :]),
                    r=[B[f"SZ{p}"], B["IDENTB"]], w=[B[f"PB{bank}"]])
        zt(0)
        for q in range(4):
            if q + 1 < 4:
                zt(q + 1)
            bank = [5, 1][q % 2]
            P.op("v", lambda v, q=q, bank=bank: v.tensor_tensor(
                vY[:, q * 512:(q + 1) * 512], vY[:, q * 512:(q + 1) * 512],
                PBb[bank][:, 0:512], ALU.mult),
                r=[B[f"PB{bank}"], B["Y"]], w=rb("Y"))
            P.op("v", lambda v, q=q: v.bn_stats(
                S["BN6"][:, q * 6:(q + 1) * 6], vY[:, q * 512:(q + 1) * 512]),
                r=rb("Y"), w=rb("BN6"))
        P.op("v", lambda v: v.bn_aggr(S["BN2R"][:, :], S["BN6"][:, 0:24]),
             r=rb("BN6"), w=rb("BN2R"))
        P.op("v", lambda v: v.scalar_tensor_tensor(
            S["RCOL"][:, 0:1], S["BN2R"][:, 0:1], S["BN2R"][:, 0:1],
            S["BN2R"][:, 1:2], ALU.mult, ALU.add), r=rb("BN2R"), w=rb("RCOL"))
        P.op("a", lambda a: a.activation(S["LNC"][:, 1:2], S["RCOL"][:, 0:1],
                                         AF.Ln, bias=S["EPSC"][:, 0:1]),
             r=rb("RCOL", "EPSC"), w=rb("LNC"))
        P.op("a", lambda a: a.activation(S["RCOL"][:, 1:2], S["LNC"][:, 1:2],
                                         AF.Exp, scale=-0.5),
             r=rb("LNC"), w=rb("RCOL"))
        for q in range(4):
            P.op("v", lambda v, q=q: v.tensor_scalar_mul(
                vYRMS[:, q * 512:(q + 1) * 512], vY[:, q * 512:(q + 1) * 512],
                S["RCOL"][:, 1:2]), r=rb("Y", "RCOL"), w=rb("YRMS"))
            bank = [5, 1][q % 2]
            for i in range(4):
                fc = 4 * q + i
                P.op("t", lambda t, i=i, fc=fc, bank=bank: t.transpose(
                    PBb[bank][:, i * 128:(i + 1) * 128],
                    vYRMS[:, fc * 128:(fc + 1) * 128], S["IDENTB"][:, :]),
                    r=rb("YRMS", "IDENTB"), w=[B[f"PB{bank}"]])
            P.op("a", lambda a, q=q, bank=bank: a.copy(
                vYFMT[:, q * 512:(q + 1) * 512], PBb[bank][:, 0:512]),
                r=[B[f"PB{bank}"]], w=rb("YFMT"))
            for i in range(4):
                f = 4 * q + i
                for jh in range(2):
                    bank2 = 6 + jh
                    P.op("t", lambda t, f=f, jh=jh, bank2=bank2: t.matmul(
                        PB[bank2][:, 0:512], vYFMT[:, f * 128:(f + 1) * 128],
                        vWOT[:, f * 1024 + jh * 512:f * 1024 + (jh + 1) * 512],
                        start=(f == 0), stop=(f == 15)),
                        r=rb("YFMT", "WOT"), w=[B[f"PB{bank2}"]])
        for jh in range(2):
            bank2 = 6 + jh
            P.op("v", lambda v, jh=jh, p=p, bank2=bank2: v.scalar_tensor_tensor(
                OUT[p][:, jh * 512:(jh + 1) * 512], PB[bank2][:, 0:512], 1.0,
                OUT[p][:, jh * 512:(jh + 1) * 512], ALU.mult, ALU.add),
                r=[B[f"PB{bank2}"], B[f"OUT{p}"]], w=[B[f"OUT{p}"]])
        t_store[c] = P.dma("a", outd[c * 128:(c + 1) * 128, :],
                           OUT[p][:, :], f"do{p}", r=[B[f"OUT{p}"]])

    def emit_outinit(c):
        p = c % 2
        xt, xb = XT[p], B[f"XT{p}"]
        P.op("a", lambda a, xt=xt, p=p: a.activation(
            OUT[p][:, :], xt[:, :], AF.Identity,
            bias=S["SCOL"][:, 8 + p:9 + p], scale=S["SCOL"][:, 6 + p:7 + p]),
            r=[xb] + rb("SCOL"), w=[B[f"OUT{p}"]])
        if c + 2 < nchunks:
            P.dma("sy", xt[:, :], xin(c + 2), f"dx{p}", w=[xb])

    def reset_hist():
        P.op("g", lambda g: g.memset(
            S["XPRE"][:, :].rearrange("p (c w) -> p c w", c=NCV)[:, :, 0:3], 0.0),
            w=[B[f"XP{i}"] for i in range(NCV)])

    emit_A1(0)
    if nchunks > 1:
        emit_A1(1)
    emit_A2v(0)
    emit_A2t(0)
    emit_outinit(0)
    if nchunks > 1:
        emit_A2v(1)
        emit_A2t(1)
        emit_outinit(1)
    if nchunks > 2:
        emit_A1(2)
    P.op("g", lambda g: g.memset(S["PREV"][:, :], 0.0), w=rb("PREV"))
    reset_hist()
    emit_B1(0)
    emit_Bconv(0)
    emit_dt(0)
    emit_Cprep(0)

    col_done = [False]

    def emit_collective():
        col_done[0] = True
        P.op("t", lambda t: t.transpose(PB[5][0:1, 192:320], S["MAXC"][:, :],
                                        S["IDENTF"][:, :]),
             r=rb("MAXC", "IDENTF"), w=rb("PB5"))
        P.op("v", lambda v: v.reduce_max(S["SX"][0:1, 0:1], PB[5][0:1, 192:320],
                                         axis=AXX), r=rb("PB5", "SX"), w=rb("SX"))
        P.dma("g", d_ccin.ap(), S["SX"][0:1, :], "dw", r=rb("SX"))
        dw_ccin = P.cnt["dw"]
        P.op("g", lambda g: g.collective_compute(
            "AllReduce", ALU.max, replica_groups=[list(range(8))],
            ins=[d_ccin.ap().opt()], outs=[d_ccout.ap().opt()]),
            after=[("dw", dw_ccin)], extra_incs=[("scc", 1)], self_inc=False)

    for c in range(nchunks):
        if c + 2 < nchunks:
            emit_A2v(c + 2)
        if c + 1 < nchunks:
            if (c + 1) % CHUNKS_PER_SEQ == 0:
                reset_hist()
            emit_B1(c + 1)
        if c + 2 < nchunks:
            emit_A2t(c + 2)
        if c + 1 < nchunks:
            emit_Bconv(c + 1)
            emit_dt(c + 1)
        emit_Cmain(c)
        if c + 1 < nchunks and (c + 1) % CHUNKS_PER_SEQ == 0:
            P.op("g", lambda g: g.memset(S["PREV"][:, :], 0.0), w=rb("PREV"))
        emit_tail(c)
        if c + 1 < nchunks:
            emit_Cprep(c + 1)
        if c + 3 < nchunks:
            emit_A1(c + 3)
            if c + 3 == nchunks - 1:
                emit_collective()
        if c + 2 < nchunks:
            emit_outinit(c + 2)

    if not col_done[0]:
        emit_collective()
    P.dma("g", vSTG[:, 0:8 * 1024].rearrange("p (a k) -> p a k", a=8),
          d_wf.rearrange("(a p) k -> p a k", p=128), "dw", w=rb("STG"))
    P.dma("sy", vWSCOL, d_ws.rearrange("(a p) one -> p (a one)", p=128), "ds",
          w=rb("WSCOL"), slow=True)
    P.dma("sy", vFBROW, d_fb.unsqueeze(0), "ds", w=rb("FBROW"))
    P.dma("g", S["GMROW"][0:1, :], d_ccout.ap(), "dw",
          after=[("scc", P.cnt["scc"])], w=rb("GMROW"))
    P.op("v", lambda v: v.reciprocal(S["SX"][0:1, 2:3], S["C127"][0:1, 0:1]),
         r=rb("C127"), w=rb("SX"))
    P.op("v", lambda v: v.tensor_scalar_mul(S["SX"][0:1, 0:1], S["GMROW"][0:1, 0:1],
                                            S["SX"][0:1, 2:3]),
         r=rb("GMROW", "SX"), w=rb("SX"))
    P.op("v", lambda v: v.tensor_scalar_max(S["SX"][0:1, 0:1], S["SX"][0:1, 0:1],
                                            EPS), r=rb("SX"), w=rb("SX"))
    P.op("v", lambda v: v.reciprocal(S["SX"][0:1, 1:2], S["SX"][0:1, 0:1]),
         r=rb("SX"), w=rb("SX"))
    P.op("t", lambda t: t.matmul(PB[5][:, 192:193], S["ONESF"][0:1, :],
                                 S["SX"][0:1, 1:2], start=True, stop=True),
         r=rb("ONESF", "SX"), w=rb("PB5"))
    P.op("v", lambda v: v.tensor_copy(S["INVSC"][:, :], PB[5][:, 192:193]),
         r=rb("PB5"), w=rb("INVSC"))
    P.op("t", lambda t: t.matmul(PB[5][:, 193:194], S["ONESF"][0:1, :],
                                 S["SX"][0:1, 0:1], start=True, stop=True),
         r=rb("ONESF", "SX"), w=rb("PB5"))
    P.op("v", lambda v: v.tensor_copy(S["SCOLS"][:, :], PB[5][:, 193:194]),
         r=rb("PB5"), w=rb("SCOLS"))
    for a in range(8):
        P.op("a", lambda aa, a=a: aa.activation(
            vSTG[:, a * 1024:(a + 1) * 1024], vSTG[:, a * 1024:(a + 1) * 1024],
            AF.Identity, scale=vWSCOL[:, a:a + 1]),
            r=rb("WSCOL"), w=rb("STG"))
    for kc in range(8):
        for ga in range(0, 8, 4):
            for j in range(4):
                a = ga + j
                P.op("t", lambda t, a=a, j=j, kc=kc: t.transpose(
                    PBb[6][:, j * 128:(j + 1) * 128],
                    vSTG[:, a * 1024 + kc * 128:a * 1024 + (kc + 1) * 128],
                    S["IDENTB"][:, :]), r=rb("STG", "IDENTB"), w=rb("PB6"))
            P.op("v", lambda v, kc=kc, ga=ga: v.tensor_copy(
                vWFT[:, kc * 1024 + ga * 128:kc * 1024 + (ga + 4) * 128],
                PBb[6][:, 0:512]), r=rb("PB6"), w=rb("WFT"))
    for half in range(2):
        P.op("t", lambda t, half=half: t.matmul(
            PB[5][:, 0:512], S["ONESF"][0:1, :],
            vFBROW[0:1, half * 512:(half + 1) * 512], start=True, stop=True),
            r=rb("ONESF", "FBROW"), w=rb("PB5"))
        P.op("v", lambda v, half=half: v.tensor_copy(
            vFBIAS[:, half * 512:(half + 1) * 512], PB[5][:, 0:512]),
            r=rb("PB5"), w=rb("FBIAS"))

    def emit_p2dma(cg):
        pp = cg % 2
        P.dma("g", XT[pp][:, :], xin(cg), f"gx{pp}", w=[B[f"XT{pp}"]])
        P.dma("sy", vPT[pp], outd[cg * 128:(cg + 1) * 128, :], f"dp{pp}",
              after=[t_store[cg]], w=[B[f"PT{pp}"]])

    def emit_quant(cg):
        pp = cg % 2
        xb = B[f"XT{pp}"]
        P.op("v", lambda v, cg=cg: v.tensor_scalar_mul(
            S["SCOL"][:, 10:11], S["RSTD"][:, cg:cg + 1], S["INVSC"][:, 0:1]),
            r=rb("RSTD", "INVSC"), w=rb("SCOL"))
        P.op("v", lambda v, cg=cg: v.tensor_scalar(
            S["SCOL"][:, 11:12], S["MU"][:, cg:cg + 1], S["SCOL"][:, 10:11],
            -1.0, ALU.mult, ALU.mult), r=rb("MU", "SCOL"), w=rb("SCOL"))
        P.op("a", lambda a, pp=pp: a.activation(
            vQS, XT[pp][:, :], AF.Identity,
            bias=S["SCOL"][:, 11:12], scale=S["SCOL"][:, 10:11]),
            r=[xb] + rb("SCOL"), w=rb("QS"))
        P.op("v", lambda v: v.tensor_scalar(
            vQS, vQS, MAGIC, MAGIC, ALU.add, ALU.subtract),
            r=rb("QS"), w=rb("QS"))
        P.op("v", lambda v: v.tensor_scalar(
            vXQ, vQS, -128.0, 127.0, ALU.max, ALU.min),
            r=rb("QS"), w=rb("XQ"))
        for gk in range(2):
            for j in range(4):
                k = gk * 4 + j
                P.op("t", lambda t, k=k, j=j: t.transpose(
                    PBb[5][:, j * 128:(j + 1) * 128], vXQ[:, k * 128:(k + 1) * 128],
                    S["IDENTB"][:, :]), r=rb("XQ", "IDENTB"), w=rb("PB5"))
            P.op("a", lambda a, gk=gk: a.copy(
                vXQT[:, gk * 512:(gk + 1) * 512], PBb[5][:, 0:512]),
                r=rb("PB5"), w=rb("XQT"))

    emit_p2dma(0)
    if nchunks > 1:
        emit_p2dma(1)
    emit_quant(0)
    for cg in range(nchunks):
        pp = cg % 2
        for k in range(8):
            for jh in range(2):
                P.op("t", lambda t, jh=jh, k=k: t.matmul(
                    PB[jh][:, 0:512], vXQT[:, k * 128:(k + 1) * 128],
                    vWFT[:, k * 1024 + jh * 512:k * 1024 + (jh + 1) * 512],
                    start=(k == 0), stop=(k == 7)),
                    r=rb("XQT", "WFT"), w=[B[f"PB{jh}"]])
        if cg + 1 < nchunks:
            emit_quant(cg + 1)
        for jh in range(2):
            P.op("v", lambda v, jh=jh, pp=pp: v.scalar_tensor_tensor(
                OUT[pp][:, jh * 512:(jh + 1) * 512], PB[jh][:, 0:512],
                S["SCOLS"][:, 0:1], vFBIAS[:, jh * 512:(jh + 1) * 512],
                ALU.mult, ALU.add),
                r=[B[f"PB{jh}"], B["SCOLS"], B["FBIAS"]], w=[B[f"OUT{pp}"]])
            P.op("v", lambda v, jh=jh, pp=pp: v.scalar_tensor_tensor(
                OUT[pp][:, jh * 512:(jh + 1) * 512],
                vPT[pp][:, jh * 512:(jh + 1) * 512], 1.0,
                OUT[pp][:, jh * 512:(jh + 1) * 512], ALU.mult, ALU.add),
                r=[B[f"PT{pp}"], B[f"OUT{pp}"]], w=[B[f"OUT{pp}"]])
        P.dma("a", outd[cg * 128:(cg + 1) * 128, :], OUT[pp][:, :], f"do{pp}",
              r=[B[f"OUT{pp}"]])
        if cg + 2 < nchunks:
            emit_p2dma(cg + 2)

    P._do("sy", lambda E: E.nop(), [("do0", P.cnt["do0"]), ("do1", P.cnt["do1"])])

    with ctx:
        with nc.Block() as block:
            P.emit(block)
    return nc


_NC_CACHE = {}


def kernel(L0_attn_x, L0_attn_norm_w, L0_attn_norm_b, L0_attn_in_proj_w,
           L0_attn_conv_w, L0_attn_conv_b, L0_attn_dt_bias, L0_attn_A_log,
           L0_attn_D, L0_attn_rms_w, L0_attn_out_proj_w, L0_ffn_weight_int8,
           L0_ffn_weight_scale, L0_ffn_bias):
    x = np.ascontiguousarray(np.asarray(L0_attn_x, dtype=np.float32))
    nb = x.shape[0]
    shard = nb // NCORES
    common = {
        "norm_w": np.ascontiguousarray(L0_attn_norm_w, dtype=np.float32),
        "norm_b": np.ascontiguousarray(L0_attn_norm_b, dtype=np.float32),
        "in_proj_w": np.ascontiguousarray(L0_attn_in_proj_w, dtype=np.float32),
        "conv_w": np.ascontiguousarray(L0_attn_conv_w, dtype=np.float32),
        "conv_b": np.ascontiguousarray(L0_attn_conv_b, dtype=np.float32),
        "dt_bias": np.ascontiguousarray(L0_attn_dt_bias, dtype=np.float32),
        "A_log": np.ascontiguousarray(L0_attn_A_log, dtype=np.float32),
        "Dp": np.ascontiguousarray(L0_attn_D, dtype=np.float32),
        "rms_w": np.ascontiguousarray(L0_attn_rms_w, dtype=np.float32),
        "out_proj_w": np.ascontiguousarray(L0_attn_out_proj_w, dtype=np.float32),
        "ffn_w": np.ascontiguousarray(L0_ffn_weight_int8, dtype=np.int8),
        "ffn_scale": np.ascontiguousarray(L0_ffn_weight_scale, dtype=np.float32),
        "ffn_bias": np.ascontiguousarray(L0_ffn_bias, dtype=np.float32),
    }
    in_maps = []
    for c in range(NCORES):
        m = dict(common)
        m["x"] = np.ascontiguousarray(
            x[c * shard:(c + 1) * shard].reshape(TOKENS, D))
        in_maps.append(m)

    if "nc" not in _NC_CACHE:
        _NC_CACHE["nc"] = build_program()
    ncx = _NC_CACHE["nc"]
    res = run_bass_kernel_spmd(ncx, in_maps, core_ids=list(range(NCORES)))
    outs = [np.asarray(res.results[c]["out"]).reshape(shard, SEQLEN, D)
            for c in range(NCORES)]
    return np.concatenate(outs, axis=0)


# version 23
# speedup vs baseline: 1.0013x; 1.0013x over previous
import contextlib
import numpy as np

import concourse.bass as bass
import concourse.mybir as mybir
from concourse.bass_utils import run_bass_kernel_spmd

F32 = mybir.dt.float32
BF16 = mybir.dt.bfloat16
U8 = mybir.dt.uint8
AF = mybir.ActivationFunctionType
ALU = mybir.AluOpType
AXX = mybir.AxisListType.X

D = 1024
DIN = 2048
NH = 32
CH = 128
NCONV = 2304
DIP = 4384
EPS = 1e-5
MAGIC = 12582912.0

NCORES = 8
SEQ_PER_CORE = 4
SEQLEN = 2048
TOKENS = SEQ_PER_CORE * SEQLEN
CHUNKS_PER_SEQ = SEQLEN // CH
NCC = 35
NCV = 18

ENGS = ("t", "v", "a", "g", "sy")


class Buf:
    __slots__ = ("w", "rs", "aliases")

    def __init__(self):
        self.w = None
        self.rs = {}
        self.aliases = []


class Plan:
    def __init__(self, sems):
        self.sems = sems
        self.items = {e: [] for e in ENGS}
        self.cnt = {name: 0 for name in sems}

    def _do(self, eng, fn, after, extra_incs=(), self_inc=True):
        sem = "s" + eng
        waits = {}
        for t in after:
            if t is None:
                continue
            s, v = t
            if s == sem and eng == "t":
                continue
            if v > waits.get(s, 0):
                waits[s] = v
        incs = []
        if self_inc:
            self.cnt[sem] += 1
            incs.append((sem, 1))
        for s2, amt in extra_incs:
            self.cnt[s2] += amt
            incs.append((s2, amt))
        self.items[eng].append((fn, waits, incs))
        if self_inc:
            return (sem, self.cnt[sem])
        return (extra_incs[0][0], self.cnt[extra_incs[0][0]])

    def op(self, eng, fn, r=(), w=(), after=(), extra_incs=(), self_inc=True):
        deps = list(after)
        for b in r:
            deps.append(b.w)
        for b in w:
            deps.append(b.w)
            deps.extend(b.rs.items())
            for ab in b.aliases:
                deps.append(ab.w)
                deps.extend(ab.rs.items())
        tok = self._do(eng, fn, deps, extra_incs, self_inc)
        for b in r:
            s, v = tok
            if v > b.rs.get(s, 0):
                b.rs[s] = v
        for b in w:
            b.w = tok
            b.rs = {}
        return tok

    def dma(self, eng, out, in_, sem, r=(), w=(), after=(), slow=False):
        def fn(E, out=out, in_=in_, slow=slow):
            return E.dma_start(out=out, in_=in_,
                               allow_slow_non_contiguous=slow)
        after = list(after) + [(sem, self.cnt[sem])]
        return self.op(eng, fn, r=r, w=w, after=after,
                       extra_incs=[(sem, 16)], self_inc=False)

    def emit(self, block):
        decos = {"t": block.tensor, "v": block.vector, "a": block.scalar,
                 "g": block.gpsimd, "sy": block.sync}
        for eng in ENGS:
            items = self.items[eng]
            if not items:
                continue

            def make(items):
                def run(E):
                    waited = {}
                    for fn, waits, incs in items:
                        for s, v in sorted(waits.items()):
                            if waited.get(s, 0) >= v:
                                continue
                            E.wait_ge(self.sems[s], v)
                            waited[s] = v
                        inst = fn(E)
                        for s, amt in incs:
                            inst = inst.then_inc(self.sems[s], amt)
                return run
            decos[eng](make(items))


def build_program():
    nchunks = TOKENS // CH
    nc = bass.Bass("TRN2", target_bir_lowering=False)

    xd = nc.dram_tensor("x", [TOKENS, D], F32, kind="ExternalInput").ap()
    d_nw = nc.dram_tensor("norm_w", [D], F32, kind="ExternalInput").ap()
    d_nb = nc.dram_tensor("norm_b", [D], F32, kind="ExternalInput").ap()
    d_wi = nc.dram_tensor("in_proj_w", [DIP, D], F32, kind="ExternalInput").ap()
    d_cw = nc.dram_tensor("conv_w", [NCONV, 4], F32, kind="ExternalInput").ap()
    d_cb = nc.dram_tensor("conv_b", [NCONV], F32, kind="ExternalInput").ap()
    d_dtb = nc.dram_tensor("dt_bias", [NH], F32, kind="ExternalInput").ap()
    d_alog = nc.dram_tensor("A_log", [NH], F32, kind="ExternalInput").ap()
    d_dd = nc.dram_tensor("Dp", [NH], F32, kind="ExternalInput").ap()
    d_rw = nc.dram_tensor("rms_w", [DIN], F32, kind="ExternalInput").ap()
    d_wo = nc.dram_tensor("out_proj_w", [D, DIN], F32, kind="ExternalInput").ap()
    d_wf = nc.dram_tensor("ffn_w", [D, D], mybir.dt.int8, kind="ExternalInput").ap()
    d_ws = nc.dram_tensor("ffn_scale", [D, 1], F32, kind="ExternalInput").ap()
    d_fb = nc.dram_tensor("ffn_bias", [D], F32, kind="ExternalInput").ap()
    outd = nc.dram_tensor("out", [TOKENS, D], F32, kind="ExternalOutput").ap()
    d_ccin = nc.dram_tensor("cc_in", [1, 8], F32)
    d_ccout = nc.dram_tensor("cc_out", [1, 8], F32)

    ctx = contextlib.ExitStack()
    S = {}

    def sbt(name, shape, dtype):
        S[name] = ctx.enter_context(nc.sbuf_tensor(name, shape, dtype))
        return S[name]

    for k in range(8):
        sbt(f"WIT{k}", [128, DIP], BF16)
    sbt("WOTA", [128, 32768], U8)
    sbt("CWDA", [128, 18432], U8)
    sbt("DDIAG", [128, 16 * 128], BF16)
    sbt("OHM", [64, NH * 128], BF16)
    sbt("IDENTF", [128, 128], F32)
    sbt("IDENTB", [128, 128], BF16)
    sbt("TRIL", [128, 128], BF16)
    sbt("ONESF", [128, 128], F32)
    sbt("BVEC", [128, NCC], F32)
    sbt("nwcol", [128, 8], F32)
    sbt("nbcolb", [128, 8], BF16)
    sbt("rwcol", [128, 16], F32)
    sbt("cwcol", [128, NCV * 4], F32)
    sbt("cbcol", [128, NCV], F32)
    sbt("dtbcol", [32, 1], F32)
    sbt("alogcol", [32, 1], F32)
    sbt("ACOL", [32, 1], F32)
    sbt("DROW", [1, 32], F32)
    sbt("DCEXP", [128, 16], F32)
    sbt("MU", [128, 64], F32)
    sbt("RSTD", [128, 64], F32)
    sbt("MAXC", [128, 1], F32)
    sbt("C127", [1, 8], F32)
    sbt("SX", [1, 8], F32)
    sbt("GMROW", [1, 8], F32)
    sbt("INVSC", [128, 1], F32)
    sbt("SCOLS", [128, 1], F32)
    sbt("ZER32", [32, CH], BF16)
    sbt("BN6", [128, 36], F32)
    sbt("BN2A", [128, 2], F32)
    sbt("BN2R", [128, 2], F32)
    sbt("LNC", [128, 4], F32)
    sbt("MXT", [128, 1], F32)
    sbt("XMN", [128, 1], F32)
    sbt("ECOL", [32, 2], F32)
    sbt("RCOL", [128, 2], F32)
    sbt("EPSC", [128, 1], F32)
    sbt("SCOL", [128, 14], F32)

    sbt("XT0", [128, 1024], F32)
    sbt("XT1", [128, 1024], F32)
    sbt("XN2T0", [128, 1024], BF16)
    sbt("XN2T1", [128, 1024], BF16)
    sbt("SZ0", [128, 2048], BF16)
    sbt("SZ1", [128, 2048], BF16)
    sbt("XBC0", [128, NCV * 128], BF16)
    sbt("XBC1", [128, NCV * 128], BF16)
    sbt("XPRE", [128, NCV * 132], BF16)
    sbt("DTF0", [32, 128], F32)
    sbt("DTF1", [32, 128], F32)
    sbt("AFM0", [32, 128], F32)
    sbt("AFM1", [32, 128], F32)
    sbt("ACUM", [32, 128], F32)
    sbt("ACUMR", [32, CH], F32)
    sbt("DDF", [32, CH], F32)
    sbt("EXPACF", [32, CH], F32)
    sbt("TMISC0", [128, 128], F32)
    sbt("TMISC1", [128, 128], F32)
    sbt("EXPTROW", [1, 32], F32)
    sbt("EXPTBC0", [128, 32], F32)
    sbt("EXPTBC1", [128, 32], F32)
    sbt("HILO0", [64, CH], BF16)
    sbt("HILO1", [64, CH], BF16)
    sbt("SM", [128, 128], F32)
    sbt("BTM", [128, 128], BF16)
    sbt("PREV", [128, DIN], F32)
    sbt("OUT0", [128, 1024], F32)
    sbt("OUT1", [128, 1024], F32)
    sbt("ARENA", [128, 17408], U8)

    ar = S["ARENA"].ap()
    arf = ar.bitcast(F32)
    arb = ar.bitcast(BF16)
    vY = arf[:, 0:2048]
    vQS = arf[:, 0:1024]
    vLBF = [arb[:, 5120:5632], arb[:, 4096:4608]]
    vMP = [arb[:, 5632:6144], arb[:, 4608:5120]]
    vXG = [arb[:, 6144:6656], arb[:, 6656:7168]]
    vXDG = arb[:, 7168:7680]
    vPVG = arb[:, 7680:8192]
    vSPS = arf[0:32, 3072:3328]
    vXN2N = arb[:, 7680:8704]
    vYRMS = arb[:, 4096:6144]
    vYFMT = arb[:, 6144:8192]
    vSTG = arb[:, 0:8192]
    vXQ = arb[:, 6144:7168]
    vXQT = arb[:, 7168:8192]
    vPT = [arf[:, 1024:2048], arf[:, 2048:3072]]

    wota = S["WOTA"].ap().bitcast(BF16)
    vWOT = wota[:, 0:16384]
    vWFT = wota[:, 0:8192]
    cwdb = S["CWDA"].ap().bitcast(BF16)
    cwdf = S["CWDA"].ap().bitcast(F32)
    vCWD = cwdb[:, 0:9216]
    vWSCOL = cwdf[:, 0:8]
    vFBROW = cwdf[0:1, 1024:2048]
    vFBIAS = cwdf[:, 3072:4096]

    PB = [ctx.enter_context(nc.psum_tensor(f"pb{i}", [128, 512], F32))
          for i in range(8)]
    PBb = [p.ap().bitcast(BF16) for p in PB]

    sems = {}
    for name in ("st", "sv", "sa", "sg", "ssy", "dx0", "dx1", "ds", "dw",
                 "do0", "do1", "dp0", "dp1", "scc", "gx0", "gx1"):
        sems[name] = ctx.enter_context(nc.semaphore(name))
    P = Plan(sems)

    B = {}
    for name in list(S.keys()):
        B[name] = Buf()
    for name in ("Y", "QS", "LBF0", "LBF1", "MP0", "MP1", "XG0", "XG1", "XDG",
                 "PVG", "XN2N", "SPS",
                 "YRMS", "YFMT", "STG", "XQ", "XQT", "PT0", "PT1",
                 "WOT", "WFT", "CWD", "WSCOL", "FBROW", "FBIAS"):
        B[name] = Buf()
    for i in range(8):
        B[f"PB{i}"] = Buf()
    for ccc in range(NCV):
        B[f"XP{ccc}"] = Buf()
    for gsl in range(4):
        B[f"PREV{gsl}"] = Buf()

    def alias(a, b):
        B[a].aliases.append(B[b])
        B[b].aliases.append(B[a])

    for nm in ("Y", "QS", "LBF0", "LBF1", "MP0", "MP1", "XG0", "XG1", "XDG",
               "PVG", "SPS", "YRMS", "YFMT", "XQ", "XQT", "PT0", "PT1", "XN2N"):
        alias("STG", nm)
    alias("QS", "Y")
    for nm in ("LBF0", "LBF1", "MP0", "MP1", "PT1"):
        alias("YRMS", nm)
    for nm in ("XG0", "XG1", "XDG", "PVG", "SPS", "XQ", "XQT", "XN2N"):
        alias("YFMT", nm)
    alias("XQ", "XG0")
    alias("XQ", "XG1")
    alias("XQ", "SPS")
    alias("XQT", "XDG")
    alias("XQT", "PVG")
    alias("XQT", "XN2N")
    alias("SPS", "XG0")
    alias("XN2N", "PVG")
    alias("PT0", "Y")
    for nm in ("LBF0", "LBF1", "MP0", "MP1"):
        alias("PT1", nm)
    alias("WOT", "WFT")
    for nm in ("WSCOL", "FBROW", "FBIAS"):
        alias("CWD", nm)

    XT = [S["XT0"], S["XT1"]]
    OUT = [S["OUT0"], S["OUT1"]]
    XN2T = [S["XN2T0"], S["XN2T1"]]
    SZ = [S["SZ0"], S["SZ1"]]
    XBC = [S["XBC0"], S["XBC1"]]
    DTF = [S["DTF0"], S["DTF1"]]
    AFM = [S["AFM0"], S["AFM1"]]

    def rb(*names):
        return [B[n] for n in names]

    P.op("g", lambda g: g.memset(S["ONESF"][:, :], 1.0), w=rb("ONESF"))
    P.op("g", lambda g: g.affine_select(
        S["IDENTF"][:, :], S["ONESF"][:, :], [[1, 128]], ALU.is_equal, 0.0,
        channel_multiplier=-1), r=rb("ONESF"), w=rb("IDENTF"))
    P.op("v", lambda v: v.tensor_copy(S["IDENTB"][:, :], S["IDENTF"][:, :]),
         r=rb("IDENTF"), w=rb("IDENTB"))
    P.op("g", lambda g: g.memset(S["TRIL"][:, :], 1.0), w=rb("TRIL"))
    P.op("g", lambda g: g.affine_select(
        S["TRIL"][:, :], S["TRIL"][:, :], [[1, 128]], ALU.is_ge, 0.0,
        channel_multiplier=-1), r=rb("TRIL"), w=rb("TRIL"))
    P.op("g", lambda g: g.memset(S["OHM"][:, :], 1.0), w=rb("OHM"))
    P.op("g", lambda g: g.affine_select(
        S["OHM"][0:32, :], S["OHM"][0:32, :], [[1, 32], [0, 128]],
        ALU.is_equal, 0.0, channel_multiplier=-1), r=rb("OHM"), w=rb("OHM"))
    P.op("g", lambda g: g.affine_select(
        S["OHM"][32:64, :], S["OHM"][32:64, :], [[1, 32], [0, 128]],
        ALU.is_equal, 0.0, channel_multiplier=-1), r=rb("OHM"), w=rb("OHM"))
    P.op("g", lambda g: g.memset(S["ZER32"][:, :], 0.0), w=rb("ZER32"))
    P.op("g", lambda g: g.memset(S["MAXC"][:, :], 0.0), w=rb("MAXC"))
    P.op("g", lambda g: g.memset(S["C127"][:, :], 127.0), w=rb("C127"))
    P.op("g", lambda g: g.memset(S["SX"][:, :], 0.0), w=rb("SX"))
    P.op("g", lambda g: g.memset(S["EPSC"][:, :], EPS), w=rb("EPSC"))

    P.dma("sy", S["nwcol"][:, :], d_nw.rearrange("(a p) -> p a", p=128), "ds",
          w=rb("nwcol"), slow=True)
    P.dma("sy", S["rwcol"][:, :], d_rw.rearrange("(a p) -> p a", p=128), "ds",
          w=rb("rwcol"), slow=True)
    P.dma("sy", S["cwcol"][:, :].rearrange("p (cc t) -> p cc t", cc=NCV),
          d_cw.rearrange("(cc p) t -> p cc t", p=128), "ds", w=rb("cwcol"),
          slow=True)
    P.dma("sy", S["cbcol"][:, :], d_cb.rearrange("(cc p) -> p cc", p=128), "ds",
          w=rb("cbcol"), slow=True)
    P.dma("sy", S["dtbcol"][:, :], d_dtb.unsqueeze(1), "ds",
          w=rb("dtbcol"), slow=True)
    P.dma("sy", S["alogcol"][:, :], d_alog.unsqueeze(1), "ds",
          w=rb("alogcol"), slow=True)
    P.dma("sy", S["DROW"][0:1, :], d_dd.unsqueeze(0), "ds", w=rb("DROW"))
    P.dma("g", S["nbcolb"][:, :], d_nb.rearrange("(a p) -> p a", p=128), "dw",
          w=rb("nbcolb"), slow=True)

    P.op("a", lambda a: a.activation(S["ACOL"][:, :], S["alogcol"][:, :], AF.Exp),
         r=rb("alogcol"), w=rb("ACOL"))
    P.op("v", lambda v: v.tensor_scalar_mul(S["ACOL"][:, :], S["ACOL"][:, :], -1.0),
         r=rb("ACOL"), w=rb("ACOL"))

    P.op("t", lambda t: t.matmul(PB[7][:, 0:32], S["ONESF"][0:1, :],
                                 S["DROW"][0:1, :], start=True, stop=True),
         r=rb("ONESF", "DROW"), w=rb("PB7"))
    P.op("v", lambda v: v.tensor_copy(S["DCEXP"][0:64, :], PB[7][0:64, 0:32:2]),
         r=rb("PB7"), w=rb("DCEXP"))
    P.op("v", lambda v: v.tensor_copy(S["DCEXP"][64:128, :], PB[7][64:128, 1:32:2]),
         r=rb("PB7"), w=rb("DCEXP"))
    for gc in range(16):
        P.op("v", lambda v, gc=gc: v.tensor_scalar_mul(
            S["DDIAG"][:, gc * 128:(gc + 1) * 128], S["IDENTB"][:, :],
            S["DCEXP"][:, gc:gc + 1]), r=rb("IDENTB", "DCEXP"), w=rb("DDIAG"))
    for cc in range(NCV):
        for tap in range(4):
            P.op("v", lambda v, cc=cc, tap=tap: v.tensor_scalar_mul(
                vCWD[:, (cc * 4 + tap) * 128:(cc * 4 + tap + 1) * 128],
                S["IDENTB"][:, :], S["cwcol"][:, cc * 4 + tap:cc * 4 + tap + 1]),
                r=rb("IDENTB", "cwcol"), w=rb("CWD"))

    wstg = wota[:, 0:8192]
    piece_ccs = [list(range(0, 8)), list(range(8, 16)),
                 list(range(16, 24)), list(range(24, 32)), list(range(32, 34))]

    def piece_dma(i):
        ccs = piece_ccs[i]
        r0, nr, na = ccs[0] * 128, len(ccs) * 128, len(ccs)
        stg = vSTG if i % 2 == 0 else wstg
        bn = "STG" if i % 2 == 0 else "WOT"
        P.dma("g", stg[:, 0:na * 1024].rearrange("p (a k) -> p a k", a=na),
              d_wi[r0:r0 + nr, :].rearrange("(a p) k -> p a k", p=128), "dw",
              w=rb(bn))

    piece_dma(0)
    piece_dma(1)
    for i, ccs in enumerate(piece_ccs):
        na = len(ccs)
        stg = vSTG if i % 2 == 0 else wstg
        bn = "STG" if i % 2 == 0 else "WOT"
        for k in range(8):
            for gi in range(0, na, 4):
                gccs = ccs[gi:gi + 4]
                n = len(gccs)
                for j, cc in enumerate(gccs):
                    a = cc - ccs[0]
                    P.op("t", lambda t, a=a, j=j, k=k, stg=stg: t.transpose(
                        PBb[5][:, j * 128:(j + 1) * 128],
                        stg[:, a * 1024 + k * 128:a * 1024 + (k + 1) * 128],
                        S["IDENTB"][:, :]),
                        r=rb(bn, "IDENTB"), w=rb("PB5"))
                P.op("v", lambda v, k=k, c0=gccs[0], n=n: v.tensor_copy(
                    S[f"WIT{k}"][:, c0 * 128:(c0 + n) * 128], PBb[5][:, 0:n * 128]),
                    r=rb("PB5"), w=rb(f"WIT{k}"))
        if i + 2 < len(piece_ccs):
            piece_dma(i + 2)
    P.dma("g", wstg[0:32, 0:1024], d_wi[4352:4384, :], "dw", w=rb("WOT"))
    for k in range(8):
        P.op("t", lambda t, k=k: t.transpose(
            PBb[5][:, 0:32], wstg[0:32, k * 128:(k + 1) * 128],
            S["IDENTB"][0:32, 0:32]), r=rb("WOT", "IDENTB"), w=rb("PB5"))
        P.op("v", lambda v, k=k: v.tensor_copy(
            S[f"WIT{k}"][:, 4352:4384], PBb[5][:, 0:32]),
            r=rb("PB5"), w=rb(f"WIT{k}"))
    P.dma("g", vSTG[:, 0:4 * 2048].rearrange("p (a f) -> p a f", a=4),
          d_wo[0:4 * 128, :].rearrange("(a p) f -> p a f", p=128), "dw",
          w=rb("STG"))
    for cc in range(NCC):
        w = 128 if cc < 34 else 32
        for k in range(8):
            P.op("t", lambda t, k=k, cc=cc, w=w: t.matmul(
                PB[7][0:w, 0:1],
                S[f"WIT{k}"][:, cc * 128:cc * 128 + w],
                S["nbcolb"][:, k:k + 1], start=(k == 0), stop=(k == 7)),
                r=rb(f"WIT{k}", "nbcolb"), w=rb("PB7"))
        P.op("v", lambda v, cc=cc, w=w: v.tensor_copy(
            S["BVEC"][0:w, cc:cc + 1], PB[7][0:w, 0:1]),
            r=rb("PB7"), w=rb("BVEC"))
    P.op("v", lambda v: v.tensor_add(S["dtbcol"][:, :], S["dtbcol"][:, :],
                                     S["BVEC"][0:32, 34:35]),
         r=rb("dtbcol", "BVEC"), w=rb("dtbcol"))
    for k in range(8):
        P.op("v", lambda v, k=k: v.tensor_scalar_mul(
            S[f"WIT{k}"][:, :], S[f"WIT{k}"][:, :], S["nwcol"][:, k:k + 1]),
            r=rb("nwcol", f"WIT{k}"), w=rb(f"WIT{k}"))

    for piece in range(2):
        a0 = piece * 4
        if piece > 0:
            P.dma("g", vSTG[:, 0:4 * 2048].rearrange("p (a f) -> p a f", a=4),
                  d_wo[a0 * 128:(a0 + 4) * 128, :].rearrange(
                      "(a p) f -> p a f", p=128), "dw", w=rb("STG"))
        for fc in range(16):
            for j in range(4):
                P.op("t", lambda t, j=j, fc=fc: t.transpose(
                    PBb[6][:, j * 128:(j + 1) * 128],
                    vSTG[:, j * 2048 + fc * 128:j * 2048 + (fc + 1) * 128],
                    S["IDENTB"][:, :]), r=rb("STG", "IDENTB"), w=rb("PB6"))
            P.op("v", lambda v, fc=fc, a0=a0: v.tensor_scalar_mul(
                vWOT[:, fc * 1024 + a0 * 128:fc * 1024 + (a0 + 4) * 128],
                PBb[6][:, 0:512], S["rwcol"][:, fc:fc + 1]),
                r=rb("PB6", "rwcol"), w=rb("WOT"))

    def xin(cg):
        return xd[cg * 128:(cg + 1) * 128, :]

    t_store = [None] * nchunks
    P.dma("sy", XT[0][:, :], xin(0), "dx0", w=rb("XT0"))
    P.dma("sy", XT[1][:, :], xin(1), "dx1", w=rb("XT1"))

    def emit_A1(c):
        p = c % 2
        xt, xb = XT[p], B[f"XT{p}"]
        P.op("v", lambda v, xt=xt: v.bn_stats(S["BN6"][:, 24:30], xt[:, 0:512]),
             r=[xb], w=rb("BN6"))
        P.op("v", lambda v, xt=xt: v.bn_stats(S["BN6"][:, 30:36], xt[:, 512:1024]),
             r=[xb], w=rb("BN6"))
        P.op("v", lambda v: v.bn_aggr(S["BN2A"][:, :], S["BN6"][:, 24:36]),
             r=rb("BN6"), w=rb("BN2A"))
        P.op("v", lambda v, c=c: v.tensor_copy(S["MU"][:, c:c + 1],
                                               S["BN2A"][:, 0:1]),
             r=rb("BN2A"), w=rb("MU"))
        P.op("a", lambda a: a.activation(S["LNC"][:, 0:1], S["BN2A"][:, 1:2],
                                         AF.Ln, bias=S["EPSC"][:, 0:1]),
             r=rb("BN2A", "EPSC"), w=rb("LNC"))
        P.op("a", lambda a, c=c: a.activation(
            S["RSTD"][:, c:c + 1], S["LNC"][:, 0:1], AF.Exp, scale=-0.5),
            r=rb("LNC"), w=rb("RSTD"))
        P.op("v", lambda v, c=c: v.tensor_scalar_mul(
            S["LNC"][:, 2:3], S["BN2A"][:, 1:2], S["RSTD"][:, c:c + 1]),
            r=rb("BN2A", "RSTD"), w=rb("LNC"))
        P.op("v", lambda v, c=c: v.tensor_scalar_mul(
            S["LNC"][:, 2:3], S["LNC"][:, 2:3], S["RSTD"][:, c:c + 1]),
            r=rb("RSTD", "LNC"), w=rb("LNC"))
        P.op("a", lambda a: a.activation(S["LNC"][:, 3:4], S["LNC"][:, 2:3],
                                         AF.Ln, bias=S["EPSC"][:, 0:1]),
             r=rb("LNC", "EPSC"), w=rb("LNC"))
        P.op("a", lambda a, p=p: a.activation(
            S["SCOL"][:, 0 + p:1 + p], S["LNC"][:, 3:4], AF.Exp, scale=-0.5),
            r=rb("LNC"), w=rb("SCOL"))
        P.op("a", lambda a, p=p: a.activation(
            S["SCOL"][:, 2 + p:3 + p], S["LNC"][:, 3:4], AF.Exp, scale=0.5),
            r=rb("LNC"), w=rb("SCOL"))
        P.op("v", lambda v, c=c, p=p: v.tensor_scalar_mul(
            S["SCOL"][:, 4 + p:5 + p], S["SCOL"][:, 0 + p:1 + p],
            S["RSTD"][:, c:c + 1]), r=rb("RSTD", "SCOL"), w=rb("SCOL"))
        P.op("v", lambda v, c=c, p=p: v.tensor_scalar_add(
            S["SCOL"][:, 6 + p:7 + p], S["RSTD"][:, c:c + 1], 1.0),
            r=rb("RSTD"), w=rb("SCOL"))
        P.op("v", lambda v, c=c, p=p: v.tensor_scalar(
            S["SCOL"][:, 8 + p:9 + p], S["MU"][:, c:c + 1],
            S["RSTD"][:, c:c + 1], -1.0, ALU.mult, ALU.mult),
            r=rb("MU", "RSTD"), w=rb("SCOL"))
        P.op("v", lambda v, c=c, p=p: v.tensor_scalar(
            S["SCOL"][:, 12 + p:13 + p], S["MU"][:, c:c + 1],
            S["SCOL"][:, 4 + p:5 + p], -1.0, ALU.mult, ALU.mult),
            r=rb("MU", "SCOL"), w=rb("SCOL"))
        P.op("v", lambda v, xt=xt: v.reduce_max(S["MXT"][:, :], xt[:, :],
                                                axis=AXX),
             r=[xb], w=rb("MXT"))
        P.op("v", lambda v, xt=xt: v.tensor_reduce(S["XMN"][:, :], xt[:, :],
                                                   op=ALU.min, axis=AXX),
             r=[xb], w=rb("XMN"))
        P.op("v", lambda v, c=c: v.tensor_scalar_sub(
            S["MXT"][:, :], S["MXT"][:, :], S["MU"][:, c:c + 1]),
            r=rb("MU", "MXT"), w=rb("MXT"))
        P.op("v", lambda v, c=c: v.tensor_scalar(
            S["XMN"][:, :], S["XMN"][:, :], S["MU"][:, c:c + 1], -1.0,
            ALU.subtract, ALU.mult), r=rb("MU", "XMN"), w=rb("XMN"))
        P.op("v", lambda v: v.tensor_max(S["MXT"][:, :], S["MXT"][:, :],
                                         S["XMN"][:, :]),
             r=rb("XMN", "MXT"), w=rb("MXT"))
        P.op("v", lambda v, c=c: v.tensor_scalar_mul(
            S["MXT"][:, :], S["MXT"][:, :], S["RSTD"][:, c:c + 1]),
            r=rb("RSTD", "MXT"), w=rb("MXT"))
        P.op("v", lambda v: v.tensor_max(S["MAXC"][:, :], S["MAXC"][:, :],
                                         S["MXT"][:, :]),
             r=rb("MXT", "MAXC"), w=rb("MAXC"))

    def emit_A2v(c):
        p = c % 2
        xt, xb = XT[p], B[f"XT{p}"]
        P.op("a", lambda a, xt=xt, p=p: a.activation(
            vXN2N, xt[:, :], AF.Identity,
            bias=S["SCOL"][:, 12 + p:13 + p], scale=S["SCOL"][:, 4 + p:5 + p]),
            r=[xb] + rb("SCOL"), w=rb("XN2N"))

    def emit_A2t(c):
        p = c % 2
        for gk in range(2):
            for j in range(4):
                k = gk * 4 + j
                P.op("t", lambda t, k=k, j=j: t.transpose(
                    PBb[3][:, j * 128:(j + 1) * 128],
                    vXN2N[:, k * 128:(k + 1) * 128], S["IDENTB"][:, :]),
                    r=rb("XN2N", "IDENTB"), w=rb("PB3"))
            P.op("a", lambda a, gk=gk, p=p: a.copy(
                XN2T[p][:, gk * 512:(gk + 1) * 512], PBb[3][:, 0:512]),
                r=rb("PB3"), w=[B[f"XN2T{p}"]])

    def emit_B1(c):
        p = c % 2
        seq_start = (c % CHUNKS_PER_SEQ == 0)
        for cc in range(NCC):
            w = 128 if cc < 34 else 32
            bank = cc % 3
            for k in range(8):
                P.op("t", lambda t, k=k, cc=cc, w=w, bank=bank, p=p: t.matmul(
                    PB[bank][0:w, 0:128], S[f"WIT{k}"][:, cc * 128:cc * 128 + w],
                    XN2T[p][:, k * 128:(k + 1) * 128],
                    start=(k == 0), stop=(k == 7)),
                    r=[B[f"WIT{k}"], B[f"XN2T{p}"]], w=[B[f"PB{bank}"]])
            if cc < 16:
                P.op("a", lambda a, cc=cc, bank=bank, p=p: a.activation(
                    SZ[p][:, cc * 128:(cc + 1) * 128], PB[bank][:, 0:128],
                    AF.Silu, bias=S["BVEC"][:, cc:cc + 1]),
                    r=[B[f"PB{bank}"], B["BVEC"]], w=[B[f"SZ{p}"]])
            elif cc < 34:
                ccc = cc - 16
                base = ccc * 132
                xpb = B[f"XP{ccc}"]
                if not seq_start:
                    P.op("v", lambda v, base=base: v.tensor_copy(
                        S["XPRE"][:, base:base + 3],
                        S["XPRE"][:, base + 128:base + 131]), w=[xpb])
                P.op("a", lambda a, base=base, cc=cc, bank=bank: a.activation(
                    S["XPRE"][:, base + 3:base + 131], PB[bank][:, 0:128],
                    AF.Identity, bias=S["BVEC"][:, cc:cc + 1]),
                    r=[B[f"PB{bank}"], B["BVEC"]], w=[xpb])

    def emit_Bconv(c):
        p = c % 2
        for ccc in range(NCV):
            base = ccc * 132
            bank = [3, 0, 2][ccc % 3]
            pbuf = B[f"PB{bank}"]
            xpb = B[f"XP{ccc}"]
            for tap in range(4):
                P.op("t", lambda t, ccc=ccc, tap=tap, base=base, bank=bank: t.matmul(
                    PB[bank][:, 0:128],
                    vCWD[:, (ccc * 4 + tap) * 128:(ccc * 4 + tap + 1) * 128],
                    S["XPRE"][:, base + tap:base + tap + 128],
                    start=(tap == 0), stop=(tap == 3)),
                    r=[xpb, B["CWD"]], w=[pbuf])
            P.op("a", lambda a, ccc=ccc, p=p, bank=bank: a.activation(
                XBC[p][:, ccc * 128:(ccc + 1) * 128], PB[bank][:, 0:128],
                AF.Silu, bias=S["cbcol"][:, ccc:ccc + 1]),
                r=[pbuf, B["cbcol"]], w=[B[f"XBC{p}"]])

    def emit_dt(c):
        p = c % 2
        sp = vSPS[:, 0:128]
        P.op("v", lambda v, p=p: v.tensor_scalar_add(
            DTF[p][:, :], PB[1][0:32, 0:128], S["dtbcol"][:, :]),
            r=[B["PB1"], B["dtbcol"]], w=[B[f"DTF{p}"]])
        P.op("v", lambda v, p=p: v.tensor_scalar_min(
            AFM[p][:, :], DTF[p][:, :], 30.0),
            r=[B[f"DTF{p}"]], w=[B[f"AFM{p}"]])
        P.op("a", lambda a, sp=sp, p=p: a.activation(
            sp, AFM[p][:, :], AF.Exp), r=[B[f"AFM{p}"]], w=rb("SPS"))
        P.op("v", lambda v, sp=sp: v.tensor_scalar_add(sp, sp, 1.0),
             r=rb("SPS"), w=rb("SPS"))
        P.op("a", lambda a, sp=sp, p=p: a.activation(
            AFM[p][:, :], sp, AF.Ln), r=rb("SPS"), w=[B[f"AFM{p}"]])
        P.op("v", lambda v, p=p: v.tensor_max(
            DTF[p][:, :], AFM[p][:, :], DTF[p][:, :]),
            r=[B[f"AFM{p}"], B[f"DTF{p}"]], w=[B[f"DTF{p}"]])
        P.op("v", lambda v, p=p: v.tensor_scalar_mul(
            AFM[p][:, :], DTF[p][:, :], S["ACOL"][:, :]),
            r=[B[f"DTF{p}"], B["ACOL"]], w=[B[f"AFM{p}"]])

    def emit_Cprep(c):
        p = c % 2
        P.op("v", lambda v, p=p: v.tensor_tensor_scan(
            S["ACUM"][:, :], AFM[p][:, :], S["ZER32"][:, :], 0.0,
            ALU.add, ALU.add), r=[B[f"AFM{p}"], B["ZER32"]], w=rb("ACUM"))
        lastc = S["ACUM"][:, 127:128]
        P.op("v", lambda v, lastc=lastc: v.tensor_scalar(
            S["DDF"][:, :], S["ACUM"][:, :], lastc, -1.0,
            ALU.subtract, ALU.mult), r=rb("ACUM"), w=rb("DDF"))
        P.op("a", lambda a: a.activation(S["EXPACF"][:, :], S["DDF"][:, :],
                                         AF.Exp), r=rb("DDF"), w=rb("EXPACF"))
        P.op("v", lambda v, p=p: v.tensor_mul(
            S["DDF"][:, :], DTF[p][:, :], S["EXPACF"][:, :]),
            r=[B[f"DTF{p}"], B["EXPACF"]], w=rb("DDF"))
        P.op("a", lambda a, lastc=lastc: a.activation(
            S["ECOL"][:, 0:1], lastc, AF.Exp), r=rb("ACUM"), w=rb("ECOL"))
        P.op("v", lambda v, lastc=lastc: v.tensor_scalar_mul(
            S["ECOL"][:, 1:2], lastc, 0.5), r=rb("ACUM"), w=rb("ECOL"))
        P.op("v", lambda v: v.tensor_scalar_sub(
            S["ACUMR"][:, :], S["ACUM"][:, :], S["ECOL"][:, 1:2]),
            r=rb("ACUM", "ECOL"), w=rb("ACUMR"))
        P.op("a", lambda a: a.activation(S["EXPACF"][:, :], S["ACUM"][:, :],
                                         AF.Exp), r=rb("ACUM"), w=rb("EXPACF"))
        srcs = [(S["ACUMR"][:, :], "ACUMR"), (DTF[p][:, :], f"DTF{p}"),
                (S["DDF"][:, :], "DDF"), (S["EXPACF"][:, :], "EXPACF")]
        for j, (src, nm) in enumerate(srcs):
            P.op("t", lambda t, j=j, src=src: t.transpose(
                PB[5][:, j * 32:(j + 1) * 32], src, S["IDENTF"][0:32, 0:32]),
                r=[B[nm], B["IDENTF"]], w=rb("PB5"))
        P.op("a", lambda a, p=p: a.copy(S[f"TMISC{p}"][:, :], PB[5][:, 0:128]),
             r=rb("PB5"), w=[B[f"TMISC{p}"]])
        P.op("t", lambda t: t.transpose(PB[5][0:1, 128:160], S["ECOL"][:, 0:1],
                                        S["IDENTF"][0:32, 0:32]),
             r=rb("ECOL", "IDENTF"), w=rb("PB5"))
        P.op("a", lambda a: a.copy(S["EXPTROW"][0:1, :], PB[5][0:1, 128:160]),
             r=rb("PB5"), w=rb("EXPTROW"))
        P.op("t", lambda t: t.matmul(PB[5][:, 160:192], S["ONESF"][0:1, :],
                                     S["EXPTROW"][0:1, :], start=True, stop=True),
             r=rb("ONESF", "EXPTROW"), w=rb("PB5"))
        P.op("a", lambda a, p=p: a.copy(S[f"EXPTBC{p}"][:, :], PB[5][:, 160:192]),
             r=rb("PB5"), w=[B[f"EXPTBC{p}"]])
        P.op("v", lambda v, p=p: v.tensor_copy(
            S[f"HILO{p}"][0:32, :], S["ACUMR"][:, :]),
            r=rb("ACUMR"), w=[B[f"HILO{p}"]])
        P.op("v", lambda v, p=p: v.tensor_sub(
            S[f"HILO{p}"][32:64, :], S["ACUMR"][:, :], S[f"HILO{p}"][0:32, :]),
            r=rb("ACUMR"), w=[B[f"HILO{p}"]])

    def emit_Cmain(c):
        p = c % 2
        xbc, xbcb = XBC[p], B[f"XBC{p}"]
        TM, TMb = S[f"TMISC{p}"], B[f"TMISC{p}"]
        HL, HLb = S[f"HILO{p}"], B[f"HILO{p}"]
        ETB, ETBb = S[f"EXPTBC{p}"], B[f"EXPTBC{p}"]
        boff = 16 * 128
        coff = 17 * 128
        P.op("t", lambda t, xbc=xbc: t.matmul(
            PB[6][:, 0:128], xbc[:, boff:boff + 128],
            xbc[:, coff:coff + 128], start=True, stop=True),
            r=[xbcb], w=rb("PB6"))
        P.op("v", lambda v: v.tensor_mul(S["SM"][:, :], PB[6][:, 0:128],
                                         S["TRIL"][:, :]),
             r=rb("PB6", "TRIL"), w=rb("SM"))
        P.op("t", lambda t, xbc=xbc: t.transpose(
            PBb[6][:, 256:384], xbc[:, boff:boff + 128], S["IDENTB"][:, :]),
            r=[xbcb, B["IDENTB"]], w=rb("PB6"))
        P.op("a", lambda a: a.copy(S["BTM"][:, :], PBb[6][:, 256:384]),
             r=rb("PB6"), w=rb("BTM"))

        xstb = ["PB5", "PB1"]
        yab = ["PB7", "PB0"]
        yob = ["PB6", "PB2"]
        stb = ["PB4", "PB3"]
        ohb = ["PB6", "PB2"]

        def xst(g):
            bank = [5, 1][g % 2]
            for i in range(4):
                xoff = (4 * g + i) * 128
                P.op("t", lambda t, xoff=xoff, i=i, xbc=xbc, bank=bank: t.transpose(
                    PBb[bank][:, i * 128:(i + 1) * 128],
                    xbc[:, xoff:xoff + 128], S["IDENTB"][:, :]),
                    r=[xbcb, B["IDENTB"]], w=[B[xstb[g % 2]]])

        def xgv(g):
            bank = [5, 1][g % 2]
            dt_b = TM[:, 32 + 8 * g:40 + 8 * g].unsqueeze(2).broadcast_to(
                [128, 8, 64])
            dd_b = TM[:, 64 + 8 * g:72 + 8 * g].unsqueeze(2).broadcast_to(
                [128, 8, 64])
            P.op("v", lambda v, dt_b=dt_b, bank=bank, g=g: v.tensor_tensor(
                vXG[g % 2].rearrange("p (h f) -> p h f", h=8),
                PBb[bank][:, 0:512].rearrange("p (h f) -> p h f", h=8),
                dt_b, ALU.mult),
                r=[B[xstb[g % 2]], TMb], w=[B[f"XG{g % 2}"]])
            P.op("v", lambda v, dd_b=dd_b, bank=bank: v.tensor_tensor(
                vXDG.rearrange("p (h f) -> p h f", h=8),
                PBb[bank][:, 0:512].rearrange("p (h f) -> p h f", h=8),
                dd_b, ALU.mult),
                r=[B[xstb[g % 2]], TMb], w=rb("XDG"))

        def states(g):
            bank = [4, 3][g % 2]
            P.op("t", lambda t, bank=bank: t.matmul(
                PB[bank][:, 0:512], S["BTM"][:, :], vXDG,
                start=True, stop=True),
                r=rb("BTM", "XDG"), w=[B[stb[g % 2]]])

        def pvg(g):
            P.op("a", lambda a, g=g: a.copy(
                vPVG, S["PREV"][:, g * 512:(g + 1) * 512]),
                r=[B[f"PREV{g}"]], w=[B["PVG"]])

        def ohm(g, sub):
            ob = [6, 2][sub]
            for j in range(4):
                h = 8 * g + 4 * sub + j
                P.op("t", lambda t, h=h, j=j, ob=ob: t.matmul(
                    PB[ob][:, j * 128:(j + 1) * 128],
                    S["OHM"][:, h * 128:(h + 1) * 128], HL[:, :],
                    start=True, stop=True),
                    r=[B["OHM"], HLb], w=[B[ohb[sub]]])

        def chain(g, sub):
            ob = [6, 2][sub]
            sp = sub
            ac_b = TM[:, 8 * g + 4 * sub:8 * g + 4 * sub + 4] \
                .unsqueeze(2).broadcast_to([128, 4, 128])
            P.op("v", lambda v, ac_b=ac_b, ob=ob: v.tensor_tensor(
                PB[ob][:, 0:512].rearrange("p (h f) -> p h f", h=4),
                PB[ob][:, 0:512].rearrange("p (h f) -> p h f", h=4),
                ac_b, ALU.subtract),
                r=[TMb], w=[B[ohb[sub]]])
            P.op("v", lambda v, ob=ob: v.tensor_scalar_min(
                PB[ob][:, 0:512], PB[ob][:, 0:512], 0.0),
                w=[B[ohb[sub]]])
            P.op("a", lambda a, ob=ob, sp=sp: a.activation(
                vLBF[sp], PB[ob][:, 0:512], AF.Exp),
                r=[B[ohb[sub]]], w=[B[f"LBF{sp}"]])
            sm_b = S["SM"][:, :].unsqueeze(1).broadcast_to([128, 4, 128])
            P.op("g", lambda g_, sm_b=sm_b, sp=sp: g_.tensor_tensor(
                vMP[sp].rearrange("p (h f) -> p h f", h=4),
                vLBF[sp].rearrange("p (h f) -> p h f", h=4), sm_b, ALU.mult),
                r=[B[f"LBF{sp}"], B["SM"]], w=[B[f"MP{sp}"]])

        def ydiag(g, sub):
            sp = sub
            for j in range(4):
                i = 4 * sub + j
                P.op("t", lambda t, i=i, j=j, g=g, sp=sp: t.matmul(
                    PB[[7, 0][g % 2]][:, i * 64:(i + 1) * 64],
                    vMP[sp][:, j * 128:(j + 1) * 128],
                    vXG[g % 2][:, i * 64:(i + 1) * 64],
                    start=(i == 0), stop=False),
                    r=[B[f"MP{sp}"], B[f"XG{g % 2}"]], w=[B[yab[g % 2]]])

        xst(0)
        xgv(0)
        states(0)
        pvg(0)
        for g in range(4):
            P.op("t", lambda t, xbc=xbc, g=g: t.matmul(
                PB[[6, 2][g % 2]][:, 0:512], xbc[:, coff:coff + 128],
                vPVG, start=True, stop=True),
                r=[xbcb, B["PVG"]], w=[B[yob[g % 2]]])
            P.op("v", lambda v, g=g: v.tensor_add(
                S["PREV"][:, g * 512:(g + 1) * 512],
                S["PREV"][:, g * 512:(g + 1) * 512], PB[[4, 3][g % 2]][:, 0:512]),
                r=[B[stb[g % 2]]], w=[B[f"PREV{g}"]])
            et_b = ETB[:, 8 * g:8 * (g + 1)].unsqueeze(2).broadcast_to(
                [128, 8, 64])
            P.op("g", lambda g_, g=g, et_b=et_b: g_.tensor_tensor(
                S["PREV"][:, g * 512:(g + 1) * 512].rearrange(
                    "p (h f) -> p h f", h=8),
                S["PREV"][:, g * 512:(g + 1) * 512].rearrange(
                    "p (h f) -> p h f", h=8), et_b, ALU.mult),
                r=[ETBb], w=[B[f"PREV{g}"]])
            ea_b = TM[:, 96 + 8 * g:104 + 8 * g].unsqueeze(2).broadcast_to(
                [128, 8, 64])
            P.op("v", lambda v, g=g, ea_b=ea_b: v.tensor_tensor(
                vY[:, g * 512:(g + 1) * 512].rearrange("p (h f) -> p h f", h=8),
                PB[[6, 2][g % 2]][:, 0:512].rearrange("p (h f) -> p h f", h=8),
                ea_b, ALU.mult), r=[B[yob[g % 2]], TMb], w=rb("Y"))
            ohm(g, 0)
            ohm(g, 1)
            if g + 1 < 4:
                xst(g + 1)
            chain(g, 0)
            chain(g, 1)
            if g + 1 < 4:
                xgv(g + 1)
                states(g + 1)
                pvg(g + 1)
            ydiag(g, 0)
            ydiag(g, 1)
            for i2 in range(4):
                fc = 4 * g + i2
                xoff = fc * 128
                P.op("t", lambda t, i2=i2, fc=fc, xoff=xoff, xbc=xbc, g=g: t.matmul(
                    PB[[7, 0][g % 2]][:, i2 * 128:(i2 + 1) * 128],
                    xbc[:, xoff:xoff + 128],
                    S["DDIAG"][:, fc * 128:(fc + 1) * 128],
                    start=False, stop=(i2 == 3)),
                    r=[xbcb, B["DDIAG"]], w=[B[yab[g % 2]]])
            P.op("v", lambda v, g=g: v.tensor_add(
                vY[:, g * 512:(g + 1) * 512], vY[:, g * 512:(g + 1) * 512],
                PB[[7, 0][g % 2]][:, 0:512]),
                r=[B[yab[g % 2]], B["Y"]], w=rb("Y"))

    def emit_tail(c):
        p = c % 2

        def zt(q):
            bank = [5, 1][q % 2]
            for i in range(4):
                zoff = (4 * q + i) * 128
                P.op("t", lambda t, i=i, zoff=zoff, p=p, bank=bank: t.transpose(
                    PBb[bank][:, i * 128:(i + 1) * 128],
                    SZ[p][:, zoff:zoff + 128], S["IDENTB"][:, :]),
                    r=[B[f"SZ{p}"], B["IDENTB"]], w=[B[f"PB{bank}"]])
        zt(0)
        for q in range(4):
            if q + 1 < 4:
                zt(q + 1)
            bank = [5, 1][q % 2]
            P.op("v", lambda v, q=q, bank=bank: v.tensor_tensor(
                vY[:, q * 512:(q + 1) * 512], vY[:, q * 512:(q + 1) * 512],
                PBb[bank][:, 0:512], ALU.mult),
                r=[B[f"PB{bank}"], B["Y"]], w=rb("Y"))
            P.op("v", lambda v, q=q: v.bn_stats(
                S["BN6"][:, q * 6:(q + 1) * 6], vY[:, q * 512:(q + 1) * 512]),
                r=rb("Y"), w=rb("BN6"))
        P.op("v", lambda v: v.bn_aggr(S["BN2R"][:, :], S["BN6"][:, 0:24]),
             r=rb("BN6"), w=rb("BN2R"))
        P.op("v", lambda v: v.scalar_tensor_tensor(
            S["RCOL"][:, 0:1], S["BN2R"][:, 0:1], S["BN2R"][:, 0:1],
            S["BN2R"][:, 1:2], ALU.mult, ALU.add), r=rb("BN2R"), w=rb("RCOL"))
        P.op("a", lambda a: a.activation(S["LNC"][:, 1:2], S["RCOL"][:, 0:1],
                                         AF.Ln, bias=S["EPSC"][:, 0:1]),
             r=rb("RCOL", "EPSC"), w=rb("LNC"))
        P.op("a", lambda a: a.activation(S["RCOL"][:, 1:2], S["LNC"][:, 1:2],
                                         AF.Exp, scale=-0.5),
             r=rb("LNC"), w=rb("RCOL"))
        for q in range(4):
            P.op("v", lambda v, q=q: v.tensor_scalar_mul(
                vYRMS[:, q * 512:(q + 1) * 512], vY[:, q * 512:(q + 1) * 512],
                S["RCOL"][:, 1:2]), r=rb("Y", "RCOL"), w=rb("YRMS"))
            bank = [5, 1][q % 2]
            for i in range(4):
                fc = 4 * q + i
                P.op("t", lambda t, i=i, fc=fc, bank=bank: t.transpose(
                    PBb[bank][:, i * 128:(i + 1) * 128],
                    vYRMS[:, fc * 128:(fc + 1) * 128], S["IDENTB"][:, :]),
                    r=rb("YRMS", "IDENTB"), w=[B[f"PB{bank}"]])
            P.op("a", lambda a, q=q, bank=bank: a.copy(
                vYFMT[:, q * 512:(q + 1) * 512], PBb[bank][:, 0:512]),
                r=[B[f"PB{bank}"]], w=rb("YFMT"))
            for i in range(4):
                f = 4 * q + i
                for jh in range(2):
                    bank2 = 6 + jh
                    P.op("t", lambda t, f=f, jh=jh, bank2=bank2: t.matmul(
                        PB[bank2][:, 0:512], vYFMT[:, f * 128:(f + 1) * 128],
                        vWOT[:, f * 1024 + jh * 512:f * 1024 + (jh + 1) * 512],
                        start=(f == 0), stop=(f == 15)),
                        r=rb("YFMT", "WOT"), w=[B[f"PB{bank2}"]])
        for jh in range(2):
            bank2 = 6 + jh
            P.op("v", lambda v, jh=jh, p=p, bank2=bank2: v.scalar_tensor_tensor(
                OUT[p][:, jh * 512:(jh + 1) * 512], PB[bank2][:, 0:512], 1.0,
                OUT[p][:, jh * 512:(jh + 1) * 512], ALU.mult, ALU.add),
                r=[B[f"PB{bank2}"], B[f"OUT{p}"]], w=[B[f"OUT{p}"]])
        t_store[c] = P.dma("a", outd[c * 128:(c + 1) * 128, :],
                           OUT[p][:, :], f"do{p}", r=[B[f"OUT{p}"]])

    def emit_outinit(c):
        p = c % 2
        xt, xb = XT[p], B[f"XT{p}"]
        P.op("a", lambda a, xt=xt, p=p: a.activation(
            OUT[p][:, :], xt[:, :], AF.Identity,
            bias=S["SCOL"][:, 8 + p:9 + p], scale=S["SCOL"][:, 6 + p:7 + p]),
            r=[xb] + rb("SCOL"), w=[B[f"OUT{p}"]])
        if c + 2 < nchunks:
            P.dma("sy", xt[:, :], xin(c + 2), f"dx{p}", w=[xb])

    def reset_hist():
        P.op("g", lambda g: g.memset(
            S["XPRE"][:, :].rearrange("p (c w) -> p c w", c=NCV)[:, :, 0:3], 0.0),
            w=[B[f"XP{i}"] for i in range(NCV)])

    emit_A1(0)
    if nchunks > 1:
        emit_A1(1)
    emit_A2v(0)
    emit_A2t(0)
    emit_outinit(0)
    if nchunks > 1:
        emit_A2v(1)
        emit_A2t(1)
        emit_outinit(1)
    if nchunks > 2:
        emit_A1(2)
    P.op("g", lambda g: g.memset(S["PREV"][:, :], 0.0),
             w=[B[f"PREV{i}"] for i in range(4)])
    reset_hist()
    emit_B1(0)
    emit_Bconv(0)
    emit_dt(0)
    emit_Cprep(0)

    col_done = [False]

    def emit_collective():
        col_done[0] = True
        P.op("t", lambda t: t.transpose(PB[5][0:1, 192:320], S["MAXC"][:, :],
                                        S["IDENTF"][:, :]),
             r=rb("MAXC", "IDENTF"), w=rb("PB5"))
        P.op("v", lambda v: v.reduce_max(S["SX"][0:1, 0:1], PB[5][0:1, 192:320],
                                         axis=AXX), r=rb("PB5", "SX"), w=rb("SX"))
        P.dma("g", d_ccin.ap(), S["SX"][0:1, :], "dw", r=rb("SX"))
        dw_ccin = P.cnt["dw"]
        P.op("g", lambda g: g.collective_compute(
            "AllReduce", ALU.max, replica_groups=[list(range(8))],
            ins=[d_ccin.ap().opt()], outs=[d_ccout.ap().opt()]),
            after=[("dw", dw_ccin)], extra_incs=[("scc", 1)], self_inc=False)

    for c in range(nchunks):
        if c + 2 < nchunks:
            emit_A2v(c + 2)
        if c + 1 < nchunks:
            if (c + 1) % CHUNKS_PER_SEQ == 0:
                reset_hist()
            emit_B1(c + 1)
        if c + 2 < nchunks:
            emit_A2t(c + 2)
        if c + 1 < nchunks:
            emit_Bconv(c + 1)
            emit_dt(c + 1)
        emit_Cmain(c)
        if c + 1 < nchunks and (c + 1) % CHUNKS_PER_SEQ == 0:
            P.op("g", lambda g: g.memset(S["PREV"][:, :], 0.0),
             w=[B[f"PREV{i}"] for i in range(4)])
        emit_tail(c)
        if c + 1 < nchunks:
            emit_Cprep(c + 1)
        if c + 3 < nchunks:
            emit_A1(c + 3)
            if c + 3 == nchunks - 1:
                emit_collective()
        if c + 2 < nchunks:
            emit_outinit(c + 2)

    if not col_done[0]:
        emit_collective()
    P.dma("g", vSTG[:, 0:8 * 1024].rearrange("p (a k) -> p a k", a=8),
          d_wf.rearrange("(a p) k -> p a k", p=128), "dw", w=rb("STG"))
    P.dma("sy", vWSCOL, d_ws.rearrange("(a p) one -> p (a one)", p=128), "ds",
          w=rb("WSCOL"), slow=True)
    P.dma("sy", vFBROW, d_fb.unsqueeze(0), "ds", w=rb("FBROW"))
    for a in range(8):
        P.op("v", lambda v, a=a: v.tensor_scalar_mul(
            vSTG[:, a * 1024:(a + 1) * 1024], vSTG[:, a * 1024:(a + 1) * 1024],
            vWSCOL[:, a:a + 1]), r=rb("WSCOL", "STG"), w=rb("STG"))
    for kc in range(8):
        for ga in range(0, 8, 4):
            for j in range(4):
                a = ga + j
                P.op("t", lambda t, a=a, j=j, kc=kc: t.transpose(
                    PBb[6][:, j * 128:(j + 1) * 128],
                    vSTG[:, a * 1024 + kc * 128:a * 1024 + (kc + 1) * 128],
                    S["IDENTB"][:, :]), r=rb("STG", "IDENTB"), w=rb("PB6"))
            P.op("v", lambda v, kc=kc, ga=ga: v.tensor_copy(
                vWFT[:, kc * 1024 + ga * 128:kc * 1024 + (ga + 4) * 128],
                PBb[6][:, 0:512]), r=rb("PB6"), w=rb("WFT"))
    P.dma("g", S["GMROW"][0:1, :], d_ccout.ap(), "dw",
          after=[("scc", P.cnt["scc"])], w=rb("GMROW"))
    P.op("v", lambda v: v.reciprocal(S["SX"][0:1, 2:3], S["C127"][0:1, 0:1]),
         r=rb("C127"), w=rb("SX"))
    P.op("v", lambda v: v.tensor_scalar_mul(S["SX"][0:1, 0:1], S["GMROW"][0:1, 0:1],
                                            S["SX"][0:1, 2:3]),
         r=rb("GMROW", "SX"), w=rb("SX"))
    P.op("v", lambda v: v.tensor_scalar_max(S["SX"][0:1, 0:1], S["SX"][0:1, 0:1],
                                            EPS), r=rb("SX"), w=rb("SX"))
    P.op("v", lambda v: v.reciprocal(S["SX"][0:1, 1:2], S["SX"][0:1, 0:1]),
         r=rb("SX"), w=rb("SX"))
    P.op("t", lambda t: t.matmul(PB[5][:, 192:193], S["ONESF"][0:1, :],
                                 S["SX"][0:1, 1:2], start=True, stop=True),
         r=rb("ONESF", "SX"), w=rb("PB5"))
    P.op("v", lambda v: v.tensor_copy(S["INVSC"][:, :], PB[5][:, 192:193]),
         r=rb("PB5"), w=rb("INVSC"))
    P.op("t", lambda t: t.matmul(PB[5][:, 193:194], S["ONESF"][0:1, :],
                                 S["SX"][0:1, 0:1], start=True, stop=True),
         r=rb("ONESF", "SX"), w=rb("PB5"))
    P.op("v", lambda v: v.tensor_copy(S["SCOLS"][:, :], PB[5][:, 193:194]),
         r=rb("PB5"), w=rb("SCOLS"))
    for half in range(2):
        P.op("t", lambda t, half=half: t.matmul(
            PB[5][:, 0:512], S["ONESF"][0:1, :],
            vFBROW[0:1, half * 512:(half + 1) * 512], start=True, stop=True),
            r=rb("ONESF", "FBROW"), w=rb("PB5"))
        P.op("v", lambda v, half=half: v.tensor_copy(
            vFBIAS[:, half * 512:(half + 1) * 512], PB[5][:, 0:512]),
            r=rb("PB5"), w=rb("FBIAS"))

    def emit_p2dma(cg):
        pp = cg % 2
        P.dma("g", XT[pp][:, :], xin(cg), f"gx{pp}", w=[B[f"XT{pp}"]])
        P.dma("sy", vPT[pp], outd[cg * 128:(cg + 1) * 128, :], f"dp{pp}",
              after=[t_store[cg]], w=[B[f"PT{pp}"]])

    def emit_quant(cg):
        pp = cg % 2
        xb = B[f"XT{pp}"]
        P.op("v", lambda v, cg=cg: v.tensor_scalar_mul(
            S["SCOL"][:, 10:11], S["RSTD"][:, cg:cg + 1], S["INVSC"][:, 0:1]),
            r=rb("RSTD", "INVSC"), w=rb("SCOL"))
        P.op("v", lambda v, cg=cg: v.tensor_scalar(
            S["SCOL"][:, 11:12], S["MU"][:, cg:cg + 1], S["SCOL"][:, 10:11],
            -1.0, ALU.mult, ALU.mult), r=rb("MU", "SCOL"), w=rb("SCOL"))
        P.op("a", lambda a, pp=pp: a.activation(
            vQS, XT[pp][:, :], AF.Identity,
            bias=S["SCOL"][:, 11:12], scale=S["SCOL"][:, 10:11]),
            r=[xb] + rb("SCOL"), w=rb("QS"))
        P.op("v", lambda v: v.tensor_scalar(
            vQS, vQS, MAGIC, MAGIC, ALU.add, ALU.subtract),
            r=rb("QS"), w=rb("QS"))
        P.op("v", lambda v: v.tensor_scalar(
            vXQ, vQS, -128.0, 127.0, ALU.max, ALU.min),
            r=rb("QS"), w=rb("XQ"))
        for gk in range(2):
            for j in range(4):
                k = gk * 4 + j
                P.op("t", lambda t, k=k, j=j: t.transpose(
                    PBb[5][:, j * 128:(j + 1) * 128], vXQ[:, k * 128:(k + 1) * 128],
                    S["IDENTB"][:, :]), r=rb("XQ", "IDENTB"), w=rb("PB5"))
            P.op("a", lambda a, gk=gk: a.copy(
                vXQT[:, gk * 512:(gk + 1) * 512], PBb[5][:, 0:512]),
                r=rb("PB5"), w=rb("XQT"))

    emit_p2dma(0)
    if nchunks > 1:
        emit_p2dma(1)
    emit_quant(0)
    for cg in range(nchunks):
        pp = cg % 2
        for jh in range(2):
            for k in range(8):
                P.op("t", lambda t, jh=jh, k=k: t.matmul(
                    PB[jh][:, 0:512], vXQT[:, k * 128:(k + 1) * 128],
                    vWFT[:, k * 1024 + jh * 512:k * 1024 + (jh + 1) * 512],
                    start=(k == 0), stop=(k == 7)),
                    r=rb("XQT", "WFT"), w=[B[f"PB{jh}"]])
        if cg + 1 < nchunks:
            emit_quant(cg + 1)
        for jh in range(2):
            P.op("v", lambda v, jh=jh, pp=pp: v.scalar_tensor_tensor(
                OUT[pp][:, jh * 512:(jh + 1) * 512], PB[jh][:, 0:512],
                S["SCOLS"][:, 0:1], vFBIAS[:, jh * 512:(jh + 1) * 512],
                ALU.mult, ALU.add),
                r=[B[f"PB{jh}"], B["SCOLS"], B["FBIAS"]], w=[B[f"OUT{pp}"]])
            P.op("v", lambda v, jh=jh, pp=pp: v.scalar_tensor_tensor(
                OUT[pp][:, jh * 512:(jh + 1) * 512],
                vPT[pp][:, jh * 512:(jh + 1) * 512], 1.0,
                OUT[pp][:, jh * 512:(jh + 1) * 512], ALU.mult, ALU.add),
                r=[B[f"PT{pp}"], B[f"OUT{pp}"]], w=[B[f"OUT{pp}"]])
        P.dma("a", outd[cg * 128:(cg + 1) * 128, :], OUT[pp][:, :], f"do{pp}",
              r=[B[f"OUT{pp}"]])
        if cg + 2 < nchunks:
            emit_p2dma(cg + 2)

    P._do("sy", lambda E: E.nop(), [("do0", P.cnt["do0"]), ("do1", P.cnt["do1"])])

    with ctx:
        with nc.Block() as block:
            P.emit(block)
    return nc


_NC_CACHE = {}


def kernel(L0_attn_x, L0_attn_norm_w, L0_attn_norm_b, L0_attn_in_proj_w,
           L0_attn_conv_w, L0_attn_conv_b, L0_attn_dt_bias, L0_attn_A_log,
           L0_attn_D, L0_attn_rms_w, L0_attn_out_proj_w, L0_ffn_weight_int8,
           L0_ffn_weight_scale, L0_ffn_bias):
    x = np.ascontiguousarray(np.asarray(L0_attn_x, dtype=np.float32))
    nb = x.shape[0]
    shard = nb // NCORES
    common = {
        "norm_w": np.ascontiguousarray(L0_attn_norm_w, dtype=np.float32),
        "norm_b": np.ascontiguousarray(L0_attn_norm_b, dtype=np.float32),
        "in_proj_w": np.ascontiguousarray(L0_attn_in_proj_w, dtype=np.float32),
        "conv_w": np.ascontiguousarray(L0_attn_conv_w, dtype=np.float32),
        "conv_b": np.ascontiguousarray(L0_attn_conv_b, dtype=np.float32),
        "dt_bias": np.ascontiguousarray(L0_attn_dt_bias, dtype=np.float32),
        "A_log": np.ascontiguousarray(L0_attn_A_log, dtype=np.float32),
        "Dp": np.ascontiguousarray(L0_attn_D, dtype=np.float32),
        "rms_w": np.ascontiguousarray(L0_attn_rms_w, dtype=np.float32),
        "out_proj_w": np.ascontiguousarray(L0_attn_out_proj_w, dtype=np.float32),
        "ffn_w": np.ascontiguousarray(L0_ffn_weight_int8, dtype=np.int8),
        "ffn_scale": np.ascontiguousarray(L0_ffn_weight_scale, dtype=np.float32),
        "ffn_bias": np.ascontiguousarray(L0_ffn_bias, dtype=np.float32),
    }
    in_maps = []
    for c in range(NCORES):
        m = dict(common)
        m["x"] = np.ascontiguousarray(
            x[c * shard:(c + 1) * shard].reshape(TOKENS, D))
        in_maps.append(m)

    if "nc" not in _NC_CACHE:
        _NC_CACHE["nc"] = build_program()
    ncx = _NC_CACHE["nc"]
    res = run_bass_kernel_spmd(ncx, in_maps, core_ids=list(range(NCORES)))
    outs = [np.asarray(res.results[c]["out"]).reshape(shard, SEQLEN, D)
            for c in range(NCORES)]
    return np.concatenate(outs, axis=0)


# version 24
# speedup vs baseline: 1.0028x; 1.0015x over previous
import contextlib
import numpy as np

import concourse.bass as bass
import concourse.mybir as mybir
from concourse.bass_utils import run_bass_kernel_spmd

F32 = mybir.dt.float32
BF16 = mybir.dt.bfloat16
U8 = mybir.dt.uint8
AF = mybir.ActivationFunctionType
ALU = mybir.AluOpType
AXX = mybir.AxisListType.X

D = 1024
DIN = 2048
NH = 32
CH = 128
NCONV = 2304
DIP = 4384
EPS = 1e-5
MAGIC = 12582912.0

NCORES = 8
SEQ_PER_CORE = 4
SEQLEN = 2048
TOKENS = SEQ_PER_CORE * SEQLEN
CHUNKS_PER_SEQ = SEQLEN // CH
NCC = 35
NCV = 18

ENGS = ("t", "v", "a", "g", "sy")


class Buf:
    __slots__ = ("w", "rs", "aliases")

    def __init__(self):
        self.w = None
        self.rs = {}
        self.aliases = []


class Plan:
    def __init__(self, sems):
        self.sems = sems
        self.items = {e: [] for e in ENGS}
        self.cnt = {name: 0 for name in sems}

    def _do(self, eng, fn, after, extra_incs=(), self_inc=True):
        sem = "s" + eng
        waits = {}
        for t in after:
            if t is None:
                continue
            s, v = t
            if s == sem and eng == "t":
                continue
            if v > waits.get(s, 0):
                waits[s] = v
        incs = []
        if self_inc:
            self.cnt[sem] += 1
            incs.append((sem, 1))
        for s2, amt in extra_incs:
            self.cnt[s2] += amt
            incs.append((s2, amt))
        self.items[eng].append((fn, waits, incs))
        if self_inc:
            return (sem, self.cnt[sem])
        return (extra_incs[0][0], self.cnt[extra_incs[0][0]])

    def op(self, eng, fn, r=(), w=(), after=(), extra_incs=(), self_inc=True):
        deps = list(after)
        for b in r:
            deps.append(b.w)
        for b in w:
            deps.append(b.w)
            deps.extend(b.rs.items())
            for ab in b.aliases:
                deps.append(ab.w)
                deps.extend(ab.rs.items())
        tok = self._do(eng, fn, deps, extra_incs, self_inc)
        for b in r:
            s, v = tok
            if v > b.rs.get(s, 0):
                b.rs[s] = v
        for b in w:
            b.w = tok
            b.rs = {}
        return tok

    def dma(self, eng, out, in_, sem, r=(), w=(), after=(), slow=False):
        def fn(E, out=out, in_=in_, slow=slow):
            return E.dma_start(out=out, in_=in_,
                               allow_slow_non_contiguous=slow)
        after = list(after) + [(sem, self.cnt[sem])]
        return self.op(eng, fn, r=r, w=w, after=after,
                       extra_incs=[(sem, 16)], self_inc=False)

    def emit(self, block):
        engsems = {"s" + e for e in ENGS}
        waited_vals = {name: set() for name in self.sems}
        for eng in ENGS:
            for fn, waits, incs in self.items[eng]:
                for s, v in waits.items():
                    waited_vals[s].add(v)
        rank = {}
        for s in engsems:
            rank[s] = {v: i + 1 for i, v in enumerate(sorted(waited_vals[s]))}

        decos = {"t": block.tensor, "v": block.vector, "a": block.scalar,
                 "g": block.gpsimd, "sy": block.sync}
        for eng in ENGS:
            items = self.items[eng]
            if not items:
                continue
            sem = "s" + eng

            def make(items, sem):
                def run(E):
                    waited = {}
                    orig = 0
                    for fn, waits, incs in items:
                        for s, v in sorted(waits.items()):
                            nv = rank[s][v] if s in engsems else v
                            if waited.get(s, 0) >= nv:
                                continue
                            E.wait_ge(self.sems[s], nv)
                            waited[s] = nv
                        inst = fn(E)
                        for s, amt in incs:
                            if s == sem:
                                orig += 1
                                if orig not in rank[sem]:
                                    continue
                            inst = inst.then_inc(self.sems[s], amt)
                return run
            decos[eng](make(items, sem))


def build_program():
    nchunks = TOKENS // CH
    nc = bass.Bass("TRN2", target_bir_lowering=False)

    xd = nc.dram_tensor("x", [TOKENS, D], F32, kind="ExternalInput").ap()
    d_nw = nc.dram_tensor("norm_w", [D], F32, kind="ExternalInput").ap()
    d_nb = nc.dram_tensor("norm_b", [D], F32, kind="ExternalInput").ap()
    d_wi = nc.dram_tensor("in_proj_w", [DIP, D], F32, kind="ExternalInput").ap()
    d_cw = nc.dram_tensor("conv_w", [NCONV, 4], F32, kind="ExternalInput").ap()
    d_cb = nc.dram_tensor("conv_b", [NCONV], F32, kind="ExternalInput").ap()
    d_dtb = nc.dram_tensor("dt_bias", [NH], F32, kind="ExternalInput").ap()
    d_alog = nc.dram_tensor("A_log", [NH], F32, kind="ExternalInput").ap()
    d_dd = nc.dram_tensor("Dp", [NH], F32, kind="ExternalInput").ap()
    d_rw = nc.dram_tensor("rms_w", [DIN], F32, kind="ExternalInput").ap()
    d_wo = nc.dram_tensor("out_proj_w", [D, DIN], F32, kind="ExternalInput").ap()
    d_wf = nc.dram_tensor("ffn_w", [D, D], mybir.dt.int8, kind="ExternalInput").ap()
    d_ws = nc.dram_tensor("ffn_scale", [D, 1], F32, kind="ExternalInput").ap()
    d_fb = nc.dram_tensor("ffn_bias", [D], F32, kind="ExternalInput").ap()
    outd = nc.dram_tensor("out", [TOKENS, D], F32, kind="ExternalOutput").ap()
    d_ccin = nc.dram_tensor("cc_in", [1, 8], F32)
    d_ccout = nc.dram_tensor("cc_out", [1, 8], F32)

    ctx = contextlib.ExitStack()
    S = {}

    def sbt(name, shape, dtype):
        S[name] = ctx.enter_context(nc.sbuf_tensor(name, shape, dtype))
        return S[name]

    for k in range(8):
        sbt(f"WIT{k}", [128, DIP], BF16)
    sbt("WOTA", [128, 32768], U8)
    sbt("CWDA", [128, 18432], U8)
    sbt("DDIAG", [128, 16 * 128], BF16)
    sbt("OHM", [64, NH * 128], BF16)
    sbt("IDENTF", [128, 128], F32)
    sbt("IDENTB", [128, 128], BF16)
    sbt("TRIL", [128, 128], BF16)
    sbt("ONESF", [128, 128], F32)
    sbt("BVEC", [128, NCC], F32)
    sbt("nwcol", [128, 8], F32)
    sbt("nbcolb", [128, 8], BF16)
    sbt("rwcol", [128, 16], F32)
    sbt("cwcol", [128, NCV * 4], F32)
    sbt("cbcol", [128, NCV], F32)
    sbt("dtbcol", [32, 1], F32)
    sbt("alogcol", [32, 1], F32)
    sbt("ACOL", [32, 1], F32)
    sbt("DROW", [1, 32], F32)
    sbt("DCEXP", [128, 16], F32)
    sbt("MU", [128, 64], F32)
    sbt("RSTD", [128, 64], F32)
    sbt("MAXC", [128, 1], F32)
    sbt("C127", [1, 8], F32)
    sbt("SX", [1, 8], F32)
    sbt("GMROW", [1, 8], F32)
    sbt("INVSC", [128, 1], F32)
    sbt("SCOLS", [128, 1], F32)
    sbt("ZER32", [32, CH], BF16)
    sbt("BN6", [128, 36], F32)
    sbt("BN2A", [128, 2], F32)
    sbt("BN2R", [128, 2], F32)
    sbt("LNC", [128, 4], F32)
    sbt("MXT", [128, 1], F32)
    sbt("XMN", [128, 1], F32)
    sbt("ECOL", [32, 2], F32)
    sbt("RCOL", [128, 2], F32)
    sbt("EPSC", [128, 1], F32)
    sbt("SCOL", [128, 14], F32)

    sbt("XT0", [128, 1024], F32)
    sbt("XT1", [128, 1024], F32)
    sbt("XN2T0", [128, 1024], BF16)
    sbt("XN2T1", [128, 1024], BF16)
    sbt("SZ0", [128, 2048], BF16)
    sbt("SZ1", [128, 2048], BF16)
    sbt("XBC0", [128, NCV * 128], BF16)
    sbt("XBC1", [128, NCV * 128], BF16)
    sbt("XPRE", [128, NCV * 132], BF16)
    sbt("DTF0", [32, 128], F32)
    sbt("DTF1", [32, 128], F32)
    sbt("AFM0", [32, 128], F32)
    sbt("AFM1", [32, 128], F32)
    sbt("ACUM", [32, 128], F32)
    sbt("ACUMR", [32, CH], F32)
    sbt("DDF", [32, CH], F32)
    sbt("EXPACF", [32, CH], F32)
    sbt("TMISC0", [128, 128], F32)
    sbt("TMISC1", [128, 128], F32)
    sbt("EXPTROW", [1, 32], F32)
    sbt("EXPTBC0", [128, 32], F32)
    sbt("EXPTBC1", [128, 32], F32)
    sbt("HILO0", [64, CH], BF16)
    sbt("HILO1", [64, CH], BF16)
    sbt("SM", [128, 128], F32)
    sbt("BTM", [128, 128], BF16)
    sbt("PREV", [128, DIN], F32)
    sbt("OUT0", [128, 1024], F32)
    sbt("OUT1", [128, 1024], F32)
    sbt("ARENA", [128, 17408], U8)

    ar = S["ARENA"].ap()
    arf = ar.bitcast(F32)
    arb = ar.bitcast(BF16)
    vY = arf[:, 0:2048]
    vQS = arf[:, 0:1024]
    vLBF = [arb[:, 5120:5632], arb[:, 4096:4608]]
    vMP = [arb[:, 5632:6144], arb[:, 4608:5120]]
    vXG = [arb[:, 6144:6656], arb[:, 6656:7168]]
    vXDG = arb[:, 7168:7680]
    vPVG = arb[:, 7680:8192]
    vSPS = arf[0:32, 3072:3328]
    vXN2N = arb[:, 7680:8704]
    vYRMS = arb[:, 4096:6144]
    vYFMT = arb[:, 6144:8192]
    vSTG = arb[:, 0:8192]
    vXQ = arb[:, 6144:7168]
    vXQT = arb[:, 7168:8192]
    vPT = [arf[:, 1024:2048], arf[:, 2048:3072]]

    wota = S["WOTA"].ap().bitcast(BF16)
    vWOT = wota[:, 0:16384]
    vWFT = wota[:, 0:8192]
    cwdb = S["CWDA"].ap().bitcast(BF16)
    cwdf = S["CWDA"].ap().bitcast(F32)
    vCWD = cwdb[:, 0:9216]
    vWSCOL = cwdf[:, 0:8]
    vFBROW = cwdf[0:1, 1024:2048]
    vFBIAS = cwdf[:, 3072:4096]

    PB = [ctx.enter_context(nc.psum_tensor(f"pb{i}", [128, 512], F32))
          for i in range(8)]
    PBb = [p.ap().bitcast(BF16) for p in PB]

    sems = {}
    for name in ("st", "sv", "sa", "sg", "ssy", "dx0", "dx1", "ds", "dw",
                 "do0", "do1", "dp0", "dp1", "scc", "gx0", "gx1"):
        sems[name] = ctx.enter_context(nc.semaphore(name))
    P = Plan(sems)

    B = {}
    for name in list(S.keys()):
        B[name] = Buf()
    for name in ("Y", "QS", "LBF0", "LBF1", "MP0", "MP1", "XG0", "XG1", "XDG",
                 "PVG", "XN2N", "SPS",
                 "YRMS", "YFMT", "STG", "XQ", "XQT", "PT0", "PT1",
                 "WOT", "WFT", "CWD", "WSCOL", "FBROW", "FBIAS"):
        B[name] = Buf()
    for i in range(8):
        B[f"PB{i}"] = Buf()
    for ccc in range(NCV):
        B[f"XP{ccc}"] = Buf()

    def alias(a, b):
        B[a].aliases.append(B[b])
        B[b].aliases.append(B[a])

    for nm in ("Y", "QS", "LBF0", "LBF1", "MP0", "MP1", "XG0", "XG1", "XDG",
               "PVG", "SPS", "YRMS", "YFMT", "XQ", "XQT", "PT0", "PT1", "XN2N"):
        alias("STG", nm)
    alias("QS", "Y")
    for nm in ("LBF0", "LBF1", "MP0", "MP1", "PT1"):
        alias("YRMS", nm)
    for nm in ("XG0", "XG1", "XDG", "PVG", "SPS", "XQ", "XQT", "XN2N"):
        alias("YFMT", nm)
    alias("XQ", "XG0")
    alias("XQ", "XG1")
    alias("XQ", "SPS")
    alias("XQT", "XDG")
    alias("XQT", "PVG")
    alias("XQT", "XN2N")
    alias("SPS", "XG0")
    alias("XN2N", "PVG")
    alias("PT0", "Y")
    for nm in ("LBF0", "LBF1", "MP0", "MP1"):
        alias("PT1", nm)
    alias("WOT", "WFT")
    for nm in ("WSCOL", "FBROW", "FBIAS"):
        alias("CWD", nm)

    XT = [S["XT0"], S["XT1"]]
    OUT = [S["OUT0"], S["OUT1"]]
    XN2T = [S["XN2T0"], S["XN2T1"]]
    SZ = [S["SZ0"], S["SZ1"]]
    XBC = [S["XBC0"], S["XBC1"]]
    DTF = [S["DTF0"], S["DTF1"]]
    AFM = [S["AFM0"], S["AFM1"]]

    def rb(*names):
        return [B[n] for n in names]

    P.op("g", lambda g: g.memset(S["ONESF"][:, :], 1.0), w=rb("ONESF"))
    P.op("g", lambda g: g.affine_select(
        S["IDENTF"][:, :], S["ONESF"][:, :], [[1, 128]], ALU.is_equal, 0.0,
        channel_multiplier=-1), r=rb("ONESF"), w=rb("IDENTF"))
    P.op("v", lambda v: v.tensor_copy(S["IDENTB"][:, :], S["IDENTF"][:, :]),
         r=rb("IDENTF"), w=rb("IDENTB"))
    P.op("g", lambda g: g.memset(S["TRIL"][:, :], 1.0), w=rb("TRIL"))
    P.op("g", lambda g: g.affine_select(
        S["TRIL"][:, :], S["TRIL"][:, :], [[1, 128]], ALU.is_ge, 0.0,
        channel_multiplier=-1), r=rb("TRIL"), w=rb("TRIL"))
    P.op("g", lambda g: g.memset(S["OHM"][:, :], 1.0), w=rb("OHM"))
    P.op("g", lambda g: g.affine_select(
        S["OHM"][0:32, :], S["OHM"][0:32, :], [[1, 32], [0, 128]],
        ALU.is_equal, 0.0, channel_multiplier=-1), r=rb("OHM"), w=rb("OHM"))
    P.op("g", lambda g: g.affine_select(
        S["OHM"][32:64, :], S["OHM"][32:64, :], [[1, 32], [0, 128]],
        ALU.is_equal, 0.0, channel_multiplier=-1), r=rb("OHM"), w=rb("OHM"))
    P.op("g", lambda g: g.memset(S["ZER32"][:, :], 0.0), w=rb("ZER32"))
    P.op("g", lambda g: g.memset(S["MAXC"][:, :], 0.0), w=rb("MAXC"))
    P.op("g", lambda g: g.memset(S["C127"][:, :], 127.0), w=rb("C127"))
    P.op("g", lambda g: g.memset(S["SX"][:, :], 0.0), w=rb("SX"))
    P.op("g", lambda g: g.memset(S["EPSC"][:, :], EPS), w=rb("EPSC"))

    P.dma("sy", S["nwcol"][:, :], d_nw.rearrange("(a p) -> p a", p=128), "ds",
          w=rb("nwcol"), slow=True)
    P.dma("sy", S["rwcol"][:, :], d_rw.rearrange("(a p) -> p a", p=128), "ds",
          w=rb("rwcol"), slow=True)
    P.dma("sy", S["cwcol"][:, :].rearrange("p (cc t) -> p cc t", cc=NCV),
          d_cw.rearrange("(cc p) t -> p cc t", p=128), "ds", w=rb("cwcol"),
          slow=True)
    P.dma("sy", S["cbcol"][:, :], d_cb.rearrange("(cc p) -> p cc", p=128), "ds",
          w=rb("cbcol"), slow=True)
    P.dma("sy", S["dtbcol"][:, :], d_dtb.unsqueeze(1), "ds",
          w=rb("dtbcol"), slow=True)
    P.dma("sy", S["alogcol"][:, :], d_alog.unsqueeze(1), "ds",
          w=rb("alogcol"), slow=True)
    P.dma("sy", S["DROW"][0:1, :], d_dd.unsqueeze(0), "ds", w=rb("DROW"))
    P.dma("g", S["nbcolb"][:, :], d_nb.rearrange("(a p) -> p a", p=128), "dw",
          w=rb("nbcolb"), slow=True)

    P.op("a", lambda a: a.activation(S["ACOL"][:, :], S["alogcol"][:, :], AF.Exp),
         r=rb("alogcol"), w=rb("ACOL"))
    P.op("v", lambda v: v.tensor_scalar_mul(S["ACOL"][:, :], S["ACOL"][:, :], -1.0),
         r=rb("ACOL"), w=rb("ACOL"))

    P.op("t", lambda t: t.matmul(PB[7][:, 0:32], S["ONESF"][0:1, :],
                                 S["DROW"][0:1, :], start=True, stop=True),
         r=rb("ONESF", "DROW"), w=rb("PB7"))
    P.op("v", lambda v: v.tensor_copy(S["DCEXP"][0:64, :], PB[7][0:64, 0:32:2]),
         r=rb("PB7"), w=rb("DCEXP"))
    P.op("v", lambda v: v.tensor_copy(S["DCEXP"][64:128, :], PB[7][64:128, 1:32:2]),
         r=rb("PB7"), w=rb("DCEXP"))
    for gc in range(16):
        P.op("v", lambda v, gc=gc: v.tensor_scalar_mul(
            S["DDIAG"][:, gc * 128:(gc + 1) * 128], S["IDENTB"][:, :],
            S["DCEXP"][:, gc:gc + 1]), r=rb("IDENTB", "DCEXP"), w=rb("DDIAG"))
    for cc in range(NCV):
        for tap in range(4):
            P.op("v", lambda v, cc=cc, tap=tap: v.tensor_scalar_mul(
                vCWD[:, (cc * 4 + tap) * 128:(cc * 4 + tap + 1) * 128],
                S["IDENTB"][:, :], S["cwcol"][:, cc * 4 + tap:cc * 4 + tap + 1]),
                r=rb("IDENTB", "cwcol"), w=rb("CWD"))

    wstg = wota[:, 0:8192]
    piece_ccs = [list(range(0, 8)), list(range(8, 16)),
                 list(range(16, 24)), list(range(24, 32)), list(range(32, 34))]

    def piece_dma(i):
        ccs = piece_ccs[i]
        r0, nr, na = ccs[0] * 128, len(ccs) * 128, len(ccs)
        stg = vSTG if i % 2 == 0 else wstg
        bn = "STG" if i % 2 == 0 else "WOT"
        P.dma("g", stg[:, 0:na * 1024].rearrange("p (a k) -> p a k", a=na),
              d_wi[r0:r0 + nr, :].rearrange("(a p) k -> p a k", p=128), "dw",
              w=rb(bn))

    piece_dma(0)
    piece_dma(1)
    for i, ccs in enumerate(piece_ccs):
        na = len(ccs)
        stg = vSTG if i % 2 == 0 else wstg
        bn = "STG" if i % 2 == 0 else "WOT"
        for k in range(8):
            for gi in range(0, na, 4):
                gccs = ccs[gi:gi + 4]
                n = len(gccs)
                for j, cc in enumerate(gccs):
                    a = cc - ccs[0]
                    P.op("t", lambda t, a=a, j=j, k=k, stg=stg: t.transpose(
                        PBb[5][:, j * 128:(j + 1) * 128],
                        stg[:, a * 1024 + k * 128:a * 1024 + (k + 1) * 128],
                        S["IDENTB"][:, :]),
                        r=rb(bn, "IDENTB"), w=rb("PB5"))
                P.op("v", lambda v, k=k, c0=gccs[0], n=n: v.tensor_copy(
                    S[f"WIT{k}"][:, c0 * 128:(c0 + n) * 128], PBb[5][:, 0:n * 128]),
                    r=rb("PB5"), w=rb(f"WIT{k}"))
        if i + 2 < len(piece_ccs):
            piece_dma(i + 2)
    P.dma("g", wstg[0:32, 0:1024], d_wi[4352:4384, :], "dw", w=rb("WOT"))
    for k in range(8):
        P.op("t", lambda t, k=k: t.transpose(
            PBb[5][:, 0:32], wstg[0:32, k * 128:(k + 1) * 128],
            S["IDENTB"][0:32, 0:32]), r=rb("WOT", "IDENTB"), w=rb("PB5"))
        P.op("v", lambda v, k=k: v.tensor_copy(
            S[f"WIT{k}"][:, 4352:4384], PBb[5][:, 0:32]),
            r=rb("PB5"), w=rb(f"WIT{k}"))
    P.dma("g", vSTG[:, 0:4 * 2048].rearrange("p (a f) -> p a f", a=4),
          d_wo[0:4 * 128, :].rearrange("(a p) f -> p a f", p=128), "dw",
          w=rb("STG"))
    for cc in range(NCC):
        w = 128 if cc < 34 else 32
        for k in range(8):
            P.op("t", lambda t, k=k, cc=cc, w=w: t.matmul(
                PB[7][0:w, 0:1],
                S[f"WIT{k}"][:, cc * 128:cc * 128 + w],
                S["nbcolb"][:, k:k + 1], start=(k == 0), stop=(k == 7)),
                r=rb(f"WIT{k}", "nbcolb"), w=rb("PB7"))
        P.op("v", lambda v, cc=cc, w=w: v.tensor_copy(
            S["BVEC"][0:w, cc:cc + 1], PB[7][0:w, 0:1]),
            r=rb("PB7"), w=rb("BVEC"))
    P.op("v", lambda v: v.tensor_add(S["dtbcol"][:, :], S["dtbcol"][:, :],
                                     S["BVEC"][0:32, 34:35]),
         r=rb("dtbcol", "BVEC"), w=rb("dtbcol"))
    for k in range(8):
        P.op("v", lambda v, k=k: v.tensor_scalar_mul(
            S[f"WIT{k}"][:, :], S[f"WIT{k}"][:, :], S["nwcol"][:, k:k + 1]),
            r=rb("nwcol", f"WIT{k}"), w=rb(f"WIT{k}"))

    for piece in range(2):
        a0 = piece * 4
        if piece > 0:
            P.dma("g", vSTG[:, 0:4 * 2048].rearrange("p (a f) -> p a f", a=4),
                  d_wo[a0 * 128:(a0 + 4) * 128, :].rearrange(
                      "(a p) f -> p a f", p=128), "dw", w=rb("STG"))
        for fc in range(16):
            for j in range(4):
                P.op("t", lambda t, j=j, fc=fc: t.transpose(
                    PBb[6][:, j * 128:(j + 1) * 128],
                    vSTG[:, j * 2048 + fc * 128:j * 2048 + (fc + 1) * 128],
                    S["IDENTB"][:, :]), r=rb("STG", "IDENTB"), w=rb("PB6"))
            P.op("v", lambda v, fc=fc, a0=a0: v.tensor_scalar_mul(
                vWOT[:, fc * 1024 + a0 * 128:fc * 1024 + (a0 + 4) * 128],
                PBb[6][:, 0:512], S["rwcol"][:, fc:fc + 1]),
                r=rb("PB6", "rwcol"), w=rb("WOT"))

    def xin(cg):
        return xd[cg * 128:(cg + 1) * 128, :]

    t_store = [None] * nchunks
    P.dma("sy", XT[0][:, :], xin(0), "dx0", w=rb("XT0"))
    P.dma("sy", XT[1][:, :], xin(1), "dx1", w=rb("XT1"))

    def emit_A1(c):
        p = c % 2
        xt, xb = XT[p], B[f"XT{p}"]
        P.op("v", lambda v, xt=xt: v.bn_stats(S["BN6"][:, 24:30], xt[:, 0:512]),
             r=[xb], w=rb("BN6"))
        P.op("v", lambda v, xt=xt: v.bn_stats(S["BN6"][:, 30:36], xt[:, 512:1024]),
             r=[xb], w=rb("BN6"))
        P.op("v", lambda v: v.bn_aggr(S["BN2A"][:, :], S["BN6"][:, 24:36]),
             r=rb("BN6"), w=rb("BN2A"))
        P.op("v", lambda v, c=c: v.tensor_copy(S["MU"][:, c:c + 1],
                                               S["BN2A"][:, 0:1]),
             r=rb("BN2A"), w=rb("MU"))
        P.op("a", lambda a: a.activation(S["LNC"][:, 0:1], S["BN2A"][:, 1:2],
                                         AF.Ln, bias=S["EPSC"][:, 0:1]),
             r=rb("BN2A", "EPSC"), w=rb("LNC"))
        P.op("a", lambda a, c=c: a.activation(
            S["RSTD"][:, c:c + 1], S["LNC"][:, 0:1], AF.Exp, scale=-0.5),
            r=rb("LNC"), w=rb("RSTD"))
        P.op("v", lambda v, c=c: v.tensor_scalar_mul(
            S["LNC"][:, 2:3], S["BN2A"][:, 1:2], S["RSTD"][:, c:c + 1]),
            r=rb("BN2A", "RSTD"), w=rb("LNC"))
        P.op("v", lambda v, c=c: v.tensor_scalar_mul(
            S["LNC"][:, 2:3], S["LNC"][:, 2:3], S["RSTD"][:, c:c + 1]),
            r=rb("RSTD", "LNC"), w=rb("LNC"))
        P.op("a", lambda a: a.activation(S["LNC"][:, 3:4], S["LNC"][:, 2:3],
                                         AF.Ln, bias=S["EPSC"][:, 0:1]),
             r=rb("LNC", "EPSC"), w=rb("LNC"))
        P.op("a", lambda a, p=p: a.activation(
            S["SCOL"][:, 0 + p:1 + p], S["LNC"][:, 3:4], AF.Exp, scale=-0.5),
            r=rb("LNC"), w=rb("SCOL"))
        P.op("a", lambda a, p=p: a.activation(
            S["SCOL"][:, 2 + p:3 + p], S["LNC"][:, 3:4], AF.Exp, scale=0.5),
            r=rb("LNC"), w=rb("SCOL"))
        P.op("v", lambda v, c=c, p=p: v.tensor_scalar_mul(
            S["SCOL"][:, 4 + p:5 + p], S["SCOL"][:, 0 + p:1 + p],
            S["RSTD"][:, c:c + 1]), r=rb("RSTD", "SCOL"), w=rb("SCOL"))
        P.op("v", lambda v, c=c, p=p: v.tensor_scalar_add(
            S["SCOL"][:, 6 + p:7 + p], S["RSTD"][:, c:c + 1], 1.0),
            r=rb("RSTD"), w=rb("SCOL"))
        P.op("v", lambda v, c=c, p=p: v.tensor_scalar(
            S["SCOL"][:, 8 + p:9 + p], S["MU"][:, c:c + 1],
            S["RSTD"][:, c:c + 1], -1.0, ALU.mult, ALU.mult),
            r=rb("MU", "RSTD"), w=rb("SCOL"))
        P.op("v", lambda v, c=c, p=p: v.tensor_scalar(
            S["SCOL"][:, 12 + p:13 + p], S["MU"][:, c:c + 1],
            S["SCOL"][:, 4 + p:5 + p], -1.0, ALU.mult, ALU.mult),
            r=rb("MU", "SCOL"), w=rb("SCOL"))
        P.op("v", lambda v, xt=xt: v.reduce_max(S["MXT"][:, :], xt[:, :],
                                                axis=AXX),
             r=[xb], w=rb("MXT"))
        P.op("v", lambda v, xt=xt: v.tensor_reduce(S["XMN"][:, :], xt[:, :],
                                                   op=ALU.min, axis=AXX),
             r=[xb], w=rb("XMN"))
        P.op("v", lambda v, c=c: v.tensor_scalar_sub(
            S["MXT"][:, :], S["MXT"][:, :], S["MU"][:, c:c + 1]),
            r=rb("MU", "MXT"), w=rb("MXT"))
        P.op("v", lambda v, c=c: v.tensor_scalar(
            S["XMN"][:, :], S["XMN"][:, :], S["MU"][:, c:c + 1], -1.0,
            ALU.subtract, ALU.mult), r=rb("MU", "XMN"), w=rb("XMN"))
        P.op("v", lambda v: v.tensor_max(S["MXT"][:, :], S["MXT"][:, :],
                                         S["XMN"][:, :]),
             r=rb("XMN", "MXT"), w=rb("MXT"))
        P.op("v", lambda v, c=c: v.tensor_scalar_mul(
            S["MXT"][:, :], S["MXT"][:, :], S["RSTD"][:, c:c + 1]),
            r=rb("RSTD", "MXT"), w=rb("MXT"))
        P.op("v", lambda v: v.tensor_max(S["MAXC"][:, :], S["MAXC"][:, :],
                                         S["MXT"][:, :]),
             r=rb("MXT", "MAXC"), w=rb("MAXC"))

    def emit_A2v(c):
        p = c % 2
        xt, xb = XT[p], B[f"XT{p}"]
        P.op("a", lambda a, xt=xt, p=p: a.activation(
            vXN2N, xt[:, :], AF.Identity,
            bias=S["SCOL"][:, 12 + p:13 + p], scale=S["SCOL"][:, 4 + p:5 + p]),
            r=[xb] + rb("SCOL"), w=rb("XN2N"))

    def emit_A2t(c):
        p = c % 2
        for gk in range(2):
            for j in range(4):
                k = gk * 4 + j
                P.op("t", lambda t, k=k, j=j: t.transpose(
                    PBb[3][:, j * 128:(j + 1) * 128],
                    vXN2N[:, k * 128:(k + 1) * 128], S["IDENTB"][:, :]),
                    r=rb("XN2N", "IDENTB"), w=rb("PB3"))
            P.op("a", lambda a, gk=gk, p=p: a.copy(
                XN2T[p][:, gk * 512:(gk + 1) * 512], PBb[3][:, 0:512]),
                r=rb("PB3"), w=[B[f"XN2T{p}"]])

    def emit_B1(c):
        p = c % 2
        seq_start = (c % CHUNKS_PER_SEQ == 0)
        for cc in range(NCC):
            w = 128 if cc < 34 else 32
            bank = cc % 3
            for k in range(8):
                P.op("t", lambda t, k=k, cc=cc, w=w, bank=bank, p=p: t.matmul(
                    PB[bank][0:w, 0:128], S[f"WIT{k}"][:, cc * 128:cc * 128 + w],
                    XN2T[p][:, k * 128:(k + 1) * 128],
                    start=(k == 0), stop=(k == 7)),
                    r=[B[f"WIT{k}"], B[f"XN2T{p}"]], w=[B[f"PB{bank}"]])
            if cc < 16:
                P.op("a", lambda a, cc=cc, bank=bank, p=p: a.activation(
                    SZ[p][:, cc * 128:(cc + 1) * 128], PB[bank][:, 0:128],
                    AF.Silu, bias=S["BVEC"][:, cc:cc + 1]),
                    r=[B[f"PB{bank}"], B["BVEC"]], w=[B[f"SZ{p}"]])
            elif cc < 34:
                ccc = cc - 16
                base = ccc * 132
                xpb = B[f"XP{ccc}"]
                if not seq_start:
                    P.op("v", lambda v, base=base: v.tensor_copy(
                        S["XPRE"][:, base:base + 3],
                        S["XPRE"][:, base + 128:base + 131]), w=[xpb])
                P.op("a", lambda a, base=base, cc=cc, bank=bank: a.activation(
                    S["XPRE"][:, base + 3:base + 131], PB[bank][:, 0:128],
                    AF.Identity, bias=S["BVEC"][:, cc:cc + 1]),
                    r=[B[f"PB{bank}"], B["BVEC"]], w=[xpb])

    def emit_Bconv(c):
        p = c % 2
        for ccc in range(NCV):
            base = ccc * 132
            bank = [3, 0, 2][ccc % 3]
            pbuf = B[f"PB{bank}"]
            xpb = B[f"XP{ccc}"]
            for tap in range(4):
                P.op("t", lambda t, ccc=ccc, tap=tap, base=base, bank=bank: t.matmul(
                    PB[bank][:, 0:128],
                    vCWD[:, (ccc * 4 + tap) * 128:(ccc * 4 + tap + 1) * 128],
                    S["XPRE"][:, base + tap:base + tap + 128],
                    start=(tap == 0), stop=(tap == 3)),
                    r=[xpb, B["CWD"]], w=[pbuf])
            P.op("a", lambda a, ccc=ccc, p=p, bank=bank: a.activation(
                XBC[p][:, ccc * 128:(ccc + 1) * 128], PB[bank][:, 0:128],
                AF.Silu, bias=S["cbcol"][:, ccc:ccc + 1]),
                r=[pbuf, B["cbcol"]], w=[B[f"XBC{p}"]])

    def emit_dt(c):
        p = c % 2
        sp = vSPS[:, 0:128]
        P.op("v", lambda v, p=p: v.tensor_scalar_add(
            DTF[p][:, :], PB[1][0:32, 0:128], S["dtbcol"][:, :]),
            r=[B["PB1"], B["dtbcol"]], w=[B[f"DTF{p}"]])
        P.op("v", lambda v, p=p: v.tensor_scalar_min(
            AFM[p][:, :], DTF[p][:, :], 30.0),
            r=[B[f"DTF{p}"]], w=[B[f"AFM{p}"]])
        P.op("a", lambda a, sp=sp, p=p: a.activation(
            sp, AFM[p][:, :], AF.Exp), r=[B[f"AFM{p}"]], w=rb("SPS"))
        P.op("v", lambda v, sp=sp: v.tensor_scalar_add(sp, sp, 1.0),
             r=rb("SPS"), w=rb("SPS"))
        P.op("a", lambda a, sp=sp, p=p: a.activation(
            AFM[p][:, :], sp, AF.Ln), r=rb("SPS"), w=[B[f"AFM{p}"]])
        P.op("v", lambda v, p=p: v.tensor_max(
            DTF[p][:, :], AFM[p][:, :], DTF[p][:, :]),
            r=[B[f"AFM{p}"], B[f"DTF{p}"]], w=[B[f"DTF{p}"]])
        P.op("v", lambda v, p=p: v.tensor_scalar_mul(
            AFM[p][:, :], DTF[p][:, :], S["ACOL"][:, :]),
            r=[B[f"DTF{p}"], B["ACOL"]], w=[B[f"AFM{p}"]])

    def emit_Cprep(c):
        p = c % 2
        P.op("v", lambda v, p=p: v.tensor_tensor_scan(
            S["ACUM"][:, :], AFM[p][:, :], S["ZER32"][:, :], 0.0,
            ALU.add, ALU.add), r=[B[f"AFM{p}"], B["ZER32"]], w=rb("ACUM"))
        lastc = S["ACUM"][:, 127:128]
        P.op("v", lambda v, lastc=lastc: v.tensor_scalar(
            S["DDF"][:, :], S["ACUM"][:, :], lastc, -1.0,
            ALU.subtract, ALU.mult), r=rb("ACUM"), w=rb("DDF"))
        P.op("a", lambda a: a.activation(S["EXPACF"][:, :], S["DDF"][:, :],
                                         AF.Exp), r=rb("DDF"), w=rb("EXPACF"))
        P.op("v", lambda v, p=p: v.tensor_mul(
            S["DDF"][:, :], DTF[p][:, :], S["EXPACF"][:, :]),
            r=[B[f"DTF{p}"], B["EXPACF"]], w=rb("DDF"))
        P.op("a", lambda a, lastc=lastc: a.activation(
            S["ECOL"][:, 0:1], lastc, AF.Exp), r=rb("ACUM"), w=rb("ECOL"))
        P.op("v", lambda v, lastc=lastc: v.tensor_scalar_mul(
            S["ECOL"][:, 1:2], lastc, 0.5), r=rb("ACUM"), w=rb("ECOL"))
        P.op("v", lambda v: v.tensor_scalar_sub(
            S["ACUMR"][:, :], S["ACUM"][:, :], S["ECOL"][:, 1:2]),
            r=rb("ACUM", "ECOL"), w=rb("ACUMR"))
        P.op("a", lambda a: a.activation(S["EXPACF"][:, :], S["ACUM"][:, :],
                                         AF.Exp), r=rb("ACUM"), w=rb("EXPACF"))
        srcs = [(S["ACUMR"][:, :], "ACUMR"), (DTF[p][:, :], f"DTF{p}"),
                (S["DDF"][:, :], "DDF"), (S["EXPACF"][:, :], "EXPACF")]
        for j, (src, nm) in enumerate(srcs):
            P.op("t", lambda t, j=j, src=src: t.transpose(
                PB[5][:, j * 32:(j + 1) * 32], src, S["IDENTF"][0:32, 0:32]),
                r=[B[nm], B["IDENTF"]], w=rb("PB5"))
        P.op("a", lambda a, p=p: a.copy(S[f"TMISC{p}"][:, :], PB[5][:, 0:128]),
             r=rb("PB5"), w=[B[f"TMISC{p}"]])
        P.op("t", lambda t: t.transpose(PB[5][0:1, 128:160], S["ECOL"][:, 0:1],
                                        S["IDENTF"][0:32, 0:32]),
             r=rb("ECOL", "IDENTF"), w=rb("PB5"))
        P.op("a", lambda a: a.copy(S["EXPTROW"][0:1, :], PB[5][0:1, 128:160]),
             r=rb("PB5"), w=rb("EXPTROW"))
        P.op("t", lambda t: t.matmul(PB[5][:, 160:192], S["ONESF"][0:1, :],
                                     S["EXPTROW"][0:1, :], start=True, stop=True),
             r=rb("ONESF", "EXPTROW"), w=rb("PB5"))
        P.op("a", lambda a, p=p: a.copy(S[f"EXPTBC{p}"][:, :], PB[5][:, 160:192]),
             r=rb("PB5"), w=[B[f"EXPTBC{p}"]])
        P.op("v", lambda v, p=p: v.tensor_copy(
            S[f"HILO{p}"][0:32, :], S["ACUMR"][:, :]),
            r=rb("ACUMR"), w=[B[f"HILO{p}"]])
        P.op("v", lambda v, p=p: v.tensor_sub(
            S[f"HILO{p}"][32:64, :], S["ACUMR"][:, :], S[f"HILO{p}"][0:32, :]),
            r=rb("ACUMR"), w=[B[f"HILO{p}"]])

    def emit_Cmain(c):
        p = c % 2
        xbc, xbcb = XBC[p], B[f"XBC{p}"]
        TM, TMb = S[f"TMISC{p}"], B[f"TMISC{p}"]
        HL, HLb = S[f"HILO{p}"], B[f"HILO{p}"]
        ETB, ETBb = S[f"EXPTBC{p}"], B[f"EXPTBC{p}"]
        boff = 16 * 128
        coff = 17 * 128
        P.op("t", lambda t, xbc=xbc: t.matmul(
            PB[6][:, 0:128], xbc[:, boff:boff + 128],
            xbc[:, coff:coff + 128], start=True, stop=True),
            r=[xbcb], w=rb("PB6"))
        P.op("v", lambda v: v.tensor_mul(S["SM"][:, :], PB[6][:, 0:128],
                                         S["TRIL"][:, :]),
             r=rb("PB6", "TRIL"), w=rb("SM"))
        P.op("t", lambda t, xbc=xbc: t.transpose(
            PBb[6][:, 256:384], xbc[:, boff:boff + 128], S["IDENTB"][:, :]),
            r=[xbcb, B["IDENTB"]], w=rb("PB6"))
        P.op("a", lambda a: a.copy(S["BTM"][:, :], PBb[6][:, 256:384]),
             r=rb("PB6"), w=rb("BTM"))

        xstb = ["PB5", "PB1"]
        yab = ["PB7", "PB0"]
        yob = ["PB6", "PB2"]
        stb = ["PB4", "PB3"]
        ohb = ["PB6", "PB2"]

        def xst(g):
            bank = [5, 1][g % 2]
            for i in range(4):
                xoff = (4 * g + i) * 128
                P.op("t", lambda t, xoff=xoff, i=i, xbc=xbc, bank=bank: t.transpose(
                    PBb[bank][:, i * 128:(i + 1) * 128],
                    xbc[:, xoff:xoff + 128], S["IDENTB"][:, :]),
                    r=[xbcb, B["IDENTB"]], w=[B[xstb[g % 2]]])

        def xgv(g):
            bank = [5, 1][g % 2]
            dt_b = TM[:, 32 + 8 * g:40 + 8 * g].unsqueeze(2).broadcast_to(
                [128, 8, 64])
            dd_b = TM[:, 64 + 8 * g:72 + 8 * g].unsqueeze(2).broadcast_to(
                [128, 8, 64])
            P.op("v", lambda v, dt_b=dt_b, bank=bank, g=g: v.tensor_tensor(
                vXG[g % 2].rearrange("p (h f) -> p h f", h=8),
                PBb[bank][:, 0:512].rearrange("p (h f) -> p h f", h=8),
                dt_b, ALU.mult),
                r=[B[xstb[g % 2]], TMb], w=[B[f"XG{g % 2}"]])
            P.op("v", lambda v, dd_b=dd_b, bank=bank: v.tensor_tensor(
                vXDG.rearrange("p (h f) -> p h f", h=8),
                PBb[bank][:, 0:512].rearrange("p (h f) -> p h f", h=8),
                dd_b, ALU.mult),
                r=[B[xstb[g % 2]], TMb], w=rb("XDG"))

        def states(g):
            bank = [4, 3][g % 2]
            P.op("t", lambda t, bank=bank: t.matmul(
                PB[bank][:, 0:512], S["BTM"][:, :], vXDG,
                start=True, stop=True),
                r=rb("BTM", "XDG"), w=[B[stb[g % 2]]])

        def pvg(g):
            P.op("a", lambda a, g=g: a.copy(
                vPVG, S["PREV"][:, g * 512:(g + 1) * 512]),
                r=rb("PREV"), w=[B["PVG"]])

        def ohm(g, sub):
            ob = [6, 2][sub]
            for j in range(4):
                h = 8 * g + 4 * sub + j
                P.op("t", lambda t, h=h, j=j, ob=ob: t.matmul(
                    PB[ob][:, j * 128:(j + 1) * 128],
                    S["OHM"][:, h * 128:(h + 1) * 128], HL[:, :],
                    start=True, stop=True),
                    r=[B["OHM"], HLb], w=[B[ohb[sub]]])

        def chain(g, sub):
            ob = [6, 2][sub]
            sp = sub
            ac_b = TM[:, 8 * g + 4 * sub:8 * g + 4 * sub + 4] \
                .unsqueeze(2).broadcast_to([128, 4, 128])
            P.op("v", lambda v, ac_b=ac_b, ob=ob: v.tensor_tensor(
                PB[ob][:, 0:512].rearrange("p (h f) -> p h f", h=4),
                PB[ob][:, 0:512].rearrange("p (h f) -> p h f", h=4),
                ac_b, ALU.subtract),
                r=[TMb], w=[B[ohb[sub]]])
            P.op("v", lambda v, ob=ob: v.tensor_scalar_min(
                PB[ob][:, 0:512], PB[ob][:, 0:512], 0.0),
                w=[B[ohb[sub]]])
            P.op("a", lambda a, ob=ob, sp=sp: a.activation(
                vLBF[sp], PB[ob][:, 0:512], AF.Exp),
                r=[B[ohb[sub]]], w=[B[f"LBF{sp}"]])
            sm_b = S["SM"][:, :].unsqueeze(1).broadcast_to([128, 4, 128])
            P.op("g", lambda g_, sm_b=sm_b, sp=sp: g_.tensor_tensor(
                vMP[sp].rearrange("p (h f) -> p h f", h=4),
                vLBF[sp].rearrange("p (h f) -> p h f", h=4), sm_b, ALU.mult),
                r=[B[f"LBF{sp}"], B["SM"]], w=[B[f"MP{sp}"]])

        def ydiag(g, sub):
            sp = sub
            for j in range(4):
                i = 4 * sub + j
                P.op("t", lambda t, i=i, j=j, g=g, sp=sp: t.matmul(
                    PB[[7, 0][g % 2]][:, i * 64:(i + 1) * 64],
                    vMP[sp][:, j * 128:(j + 1) * 128],
                    vXG[g % 2][:, i * 64:(i + 1) * 64],
                    start=(i == 0), stop=False),
                    r=[B[f"MP{sp}"], B[f"XG{g % 2}"]], w=[B[yab[g % 2]]])

        xst(0)
        xgv(0)
        states(0)
        pvg(0)
        for g in range(4):
            P.op("t", lambda t, xbc=xbc, g=g: t.matmul(
                PB[[6, 2][g % 2]][:, 0:512], xbc[:, coff:coff + 128],
                vPVG, start=True, stop=True),
                r=[xbcb, B["PVG"]], w=[B[yob[g % 2]]])
            P.op("v", lambda v, g=g: v.tensor_add(
                S["PREV"][:, g * 512:(g + 1) * 512],
                S["PREV"][:, g * 512:(g + 1) * 512], PB[[4, 3][g % 2]][:, 0:512]),
                r=[B[stb[g % 2]], B["PREV"]], w=rb("PREV"))
            et_b = ETB[:, 8 * g:8 * (g + 1)].unsqueeze(2).broadcast_to(
                [128, 8, 64])
            P.op("g", lambda g_, g=g, et_b=et_b: g_.tensor_tensor(
                S["PREV"][:, g * 512:(g + 1) * 512].rearrange(
                    "p (h f) -> p h f", h=8),
                S["PREV"][:, g * 512:(g + 1) * 512].rearrange(
                    "p (h f) -> p h f", h=8), et_b, ALU.mult),
                r=[ETBb, B["PREV"]], w=rb("PREV"))
            ea_b = TM[:, 96 + 8 * g:104 + 8 * g].unsqueeze(2).broadcast_to(
                [128, 8, 64])
            P.op("v", lambda v, g=g, ea_b=ea_b: v.tensor_tensor(
                vY[:, g * 512:(g + 1) * 512].rearrange("p (h f) -> p h f", h=8),
                PB[[6, 2][g % 2]][:, 0:512].rearrange("p (h f) -> p h f", h=8),
                ea_b, ALU.mult), r=[B[yob[g % 2]], TMb], w=rb("Y"))
            ohm(g, 0)
            ohm(g, 1)
            if g + 1 < 4:
                xst(g + 1)
            chain(g, 0)
            chain(g, 1)
            if g + 1 < 4:
                xgv(g + 1)
                states(g + 1)
                pvg(g + 1)
            ydiag(g, 0)
            ydiag(g, 1)
            for i2 in range(4):
                fc = 4 * g + i2
                xoff = fc * 128
                P.op("t", lambda t, i2=i2, fc=fc, xoff=xoff, xbc=xbc, g=g: t.matmul(
                    PB[[7, 0][g % 2]][:, i2 * 128:(i2 + 1) * 128],
                    xbc[:, xoff:xoff + 128],
                    S["DDIAG"][:, fc * 128:(fc + 1) * 128],
                    start=False, stop=(i2 == 3)),
                    r=[xbcb, B["DDIAG"]], w=[B[yab[g % 2]]])
            P.op("v", lambda v, g=g: v.tensor_add(
                vY[:, g * 512:(g + 1) * 512], vY[:, g * 512:(g + 1) * 512],
                PB[[7, 0][g % 2]][:, 0:512]),
                r=[B[yab[g % 2]], B["Y"]], w=rb("Y"))

    def emit_tail(c):
        p = c % 2

        def zt(q):
            bank = [5, 1][q % 2]
            for i in range(4):
                zoff = (4 * q + i) * 128
                P.op("t", lambda t, i=i, zoff=zoff, p=p, bank=bank: t.transpose(
                    PBb[bank][:, i * 128:(i + 1) * 128],
                    SZ[p][:, zoff:zoff + 128], S["IDENTB"][:, :]),
                    r=[B[f"SZ{p}"], B["IDENTB"]], w=[B[f"PB{bank}"]])
        zt(0)
        for q in range(4):
            if q + 1 < 4:
                zt(q + 1)
            bank = [5, 1][q % 2]
            P.op("v", lambda v, q=q, bank=bank: v.tensor_tensor(
                vY[:, q * 512:(q + 1) * 512], vY[:, q * 512:(q + 1) * 512],
                PBb[bank][:, 0:512], ALU.mult),
                r=[B[f"PB{bank}"], B["Y"]], w=rb("Y"))
            P.op("v", lambda v, q=q: v.bn_stats(
                S["BN6"][:, q * 6:(q + 1) * 6], vY[:, q * 512:(q + 1) * 512]),
                r=rb("Y"), w=rb("BN6"))
        P.op("v", lambda v: v.bn_aggr(S["BN2R"][:, :], S["BN6"][:, 0:24]),
             r=rb("BN6"), w=rb("BN2R"))
        P.op("v", lambda v: v.scalar_tensor_tensor(
            S["RCOL"][:, 0:1], S["BN2R"][:, 0:1], S["BN2R"][:, 0:1],
            S["BN2R"][:, 1:2], ALU.mult, ALU.add), r=rb("BN2R"), w=rb("RCOL"))
        P.op("a", lambda a: a.activation(S["LNC"][:, 1:2], S["RCOL"][:, 0:1],
                                         AF.Ln, bias=S["EPSC"][:, 0:1]),
             r=rb("RCOL", "EPSC"), w=rb("LNC"))
        P.op("a", lambda a: a.activation(S["RCOL"][:, 1:2], S["LNC"][:, 1:2],
                                         AF.Exp, scale=-0.5),
             r=rb("LNC"), w=rb("RCOL"))
        for q in range(4):
            P.op("v", lambda v, q=q: v.tensor_scalar_mul(
                vYRMS[:, q * 512:(q + 1) * 512], vY[:, q * 512:(q + 1) * 512],
                S["RCOL"][:, 1:2]), r=rb("Y", "RCOL"), w=rb("YRMS"))
            bank = [5, 1][q % 2]
            for i in range(4):
                fc = 4 * q + i
                P.op("t", lambda t, i=i, fc=fc, bank=bank: t.transpose(
                    PBb[bank][:, i * 128:(i + 1) * 128],
                    vYRMS[:, fc * 128:(fc + 1) * 128], S["IDENTB"][:, :]),
                    r=rb("YRMS", "IDENTB"), w=[B[f"PB{bank}"]])
            P.op("a", lambda a, q=q, bank=bank: a.copy(
                vYFMT[:, q * 512:(q + 1) * 512], PBb[bank][:, 0:512]),
                r=[B[f"PB{bank}"]], w=rb("YFMT"))
            for i in range(4):
                f = 4 * q + i
                for jh in range(2):
                    bank2 = 6 + jh
                    P.op("t", lambda t, f=f, jh=jh, bank2=bank2: t.matmul(
                        PB[bank2][:, 0:512], vYFMT[:, f * 128:(f + 1) * 128],
                        vWOT[:, f * 1024 + jh * 512:f * 1024 + (jh + 1) * 512],
                        start=(f == 0), stop=(f == 15)),
                        r=rb("YFMT", "WOT"), w=[B[f"PB{bank2}"]])
        for jh in range(2):
            bank2 = 6 + jh
            P.op("v", lambda v, jh=jh, p=p, bank2=bank2: v.scalar_tensor_tensor(
                OUT[p][:, jh * 512:(jh + 1) * 512], PB[bank2][:, 0:512], 1.0,
                OUT[p][:, jh * 512:(jh + 1) * 512], ALU.mult, ALU.add),
                r=[B[f"PB{bank2}"], B[f"OUT{p}"]], w=[B[f"OUT{p}"]])
        t_store[c] = P.dma("a", outd[c * 128:(c + 1) * 128, :],
                           OUT[p][:, :], f"do{p}", r=[B[f"OUT{p}"]])

    def emit_outinit(c):
        p = c % 2
        xt, xb = XT[p], B[f"XT{p}"]
        P.op("a", lambda a, xt=xt, p=p: a.activation(
            OUT[p][:, :], xt[:, :], AF.Identity,
            bias=S["SCOL"][:, 8 + p:9 + p], scale=S["SCOL"][:, 6 + p:7 + p]),
            r=[xb] + rb("SCOL"), w=[B[f"OUT{p}"]])
        if c + 2 < nchunks:
            P.dma("sy", xt[:, :], xin(c + 2), f"dx{p}", w=[xb])

    def reset_hist():
        P.op("g", lambda g: g.memset(
            S["XPRE"][:, :].rearrange("p (c w) -> p c w", c=NCV)[:, :, 0:3], 0.0),
            w=[B[f"XP{i}"] for i in range(NCV)])

    emit_A1(0)
    if nchunks > 1:
        emit_A1(1)
    emit_A2v(0)
    emit_A2t(0)
    emit_outinit(0)
    if nchunks > 1:
        emit_A2v(1)
        emit_A2t(1)
        emit_outinit(1)
    if nchunks > 2:
        emit_A1(2)
    P.op("g", lambda g: g.memset(S["PREV"][:, :], 0.0), w=rb("PREV"))
    reset_hist()
    emit_B1(0)
    emit_Bconv(0)
    emit_dt(0)
    emit_Cprep(0)

    col_done = [False]

    def emit_collective():
        col_done[0] = True
        P.op("t", lambda t: t.transpose(PB[5][0:1, 192:320], S["MAXC"][:, :],
                                        S["IDENTF"][:, :]),
             r=rb("MAXC", "IDENTF"), w=rb("PB5"))
        P.op("v", lambda v: v.reduce_max(S["SX"][0:1, 0:1], PB[5][0:1, 192:320],
                                         axis=AXX), r=rb("PB5", "SX"), w=rb("SX"))
        P.dma("g", d_ccin.ap(), S["SX"][0:1, :], "dw", r=rb("SX"))
        dw_ccin = P.cnt["dw"]
        P.op("g", lambda g: g.collective_compute(
            "AllReduce", ALU.max, replica_groups=[list(range(8))],
            ins=[d_ccin.ap().opt()], outs=[d_ccout.ap().opt()]),
            after=[("dw", dw_ccin)], extra_incs=[("scc", 1)], self_inc=False)

    for c in range(nchunks):
        if c + 2 < nchunks:
            emit_A2v(c + 2)
        if c + 1 < nchunks:
            if (c + 1) % CHUNKS_PER_SEQ == 0:
                reset_hist()
            emit_B1(c + 1)
        if c + 2 < nchunks:
            emit_A2t(c + 2)
        if c + 1 < nchunks:
            emit_Bconv(c + 1)
            emit_dt(c + 1)
        emit_Cmain(c)
        if c + 1 < nchunks and (c + 1) % CHUNKS_PER_SEQ == 0:
            P.op("g", lambda g: g.memset(S["PREV"][:, :], 0.0), w=rb("PREV"))
        emit_tail(c)
        if c + 1 < nchunks:
            emit_Cprep(c + 1)
        if c + 3 < nchunks:
            emit_A1(c + 3)
            if c + 3 == nchunks - 1:
                emit_collective()
        if c + 2 < nchunks:
            emit_outinit(c + 2)

    if not col_done[0]:
        emit_collective()
    P.dma("g", vSTG[:, 0:8 * 1024].rearrange("p (a k) -> p a k", a=8),
          d_wf.rearrange("(a p) k -> p a k", p=128), "dw", w=rb("STG"))
    P.dma("sy", vWSCOL, d_ws.rearrange("(a p) one -> p (a one)", p=128), "ds",
          w=rb("WSCOL"), slow=True)
    P.dma("sy", vFBROW, d_fb.unsqueeze(0), "ds", w=rb("FBROW"))
    for a in range(8):
        P.op("v", lambda v, a=a: v.tensor_scalar_mul(
            vSTG[:, a * 1024:(a + 1) * 1024], vSTG[:, a * 1024:(a + 1) * 1024],
            vWSCOL[:, a:a + 1]), r=rb("WSCOL", "STG"), w=rb("STG"))
    for kc in range(8):
        for ga in range(0, 8, 4):
            for j in range(4):
                a = ga + j
                P.op("t", lambda t, a=a, j=j, kc=kc: t.transpose(
                    PBb[6][:, j * 128:(j + 1) * 128],
                    vSTG[:, a * 1024 + kc * 128:a * 1024 + (kc + 1) * 128],
                    S["IDENTB"][:, :]), r=rb("STG", "IDENTB"), w=rb("PB6"))
            P.op("v", lambda v, kc=kc, ga=ga: v.tensor_copy(
                vWFT[:, kc * 1024 + ga * 128:kc * 1024 + (ga + 4) * 128],
                PBb[6][:, 0:512]), r=rb("PB6"), w=rb("WFT"))
    P.dma("g", S["GMROW"][0:1, :], d_ccout.ap(), "dw",
          after=[("scc", P.cnt["scc"])], w=rb("GMROW"))
    P.op("v", lambda v: v.reciprocal(S["SX"][0:1, 2:3], S["C127"][0:1, 0:1]),
         r=rb("C127"), w=rb("SX"))
    P.op("v", lambda v: v.tensor_scalar_mul(S["SX"][0:1, 0:1], S["GMROW"][0:1, 0:1],
                                            S["SX"][0:1, 2:3]),
         r=rb("GMROW", "SX"), w=rb("SX"))
    P.op("v", lambda v: v.tensor_scalar_max(S["SX"][0:1, 0:1], S["SX"][0:1, 0:1],
                                            EPS), r=rb("SX"), w=rb("SX"))
    P.op("v", lambda v: v.reciprocal(S["SX"][0:1, 1:2], S["SX"][0:1, 0:1]),
         r=rb("SX"), w=rb("SX"))
    P.op("t", lambda t: t.matmul(PB[5][:, 192:193], S["ONESF"][0:1, :],
                                 S["SX"][0:1, 1:2], start=True, stop=True),
         r=rb("ONESF", "SX"), w=rb("PB5"))
    P.op("v", lambda v: v.tensor_copy(S["INVSC"][:, :], PB[5][:, 192:193]),
         r=rb("PB5"), w=rb("INVSC"))
    P.op("t", lambda t: t.matmul(PB[5][:, 193:194], S["ONESF"][0:1, :],
                                 S["SX"][0:1, 0:1], start=True, stop=True),
         r=rb("ONESF", "SX"), w=rb("PB5"))
    P.op("v", lambda v: v.tensor_copy(S["SCOLS"][:, :], PB[5][:, 193:194]),
         r=rb("PB5"), w=rb("SCOLS"))
    for half in range(2):
        P.op("t", lambda t, half=half: t.matmul(
            PB[5][:, 0:512], S["ONESF"][0:1, :],
            vFBROW[0:1, half * 512:(half + 1) * 512], start=True, stop=True),
            r=rb("ONESF", "FBROW"), w=rb("PB5"))
        P.op("v", lambda v, half=half: v.tensor_copy(
            vFBIAS[:, half * 512:(half + 1) * 512], PB[5][:, 0:512]),
            r=rb("PB5"), w=rb("FBIAS"))

    def emit_p2dma(cg):
        pp = cg % 2
        P.dma("g", XT[pp][:, :], xin(cg), f"gx{pp}", w=[B[f"XT{pp}"]])
        P.dma("sy", vPT[pp], outd[cg * 128:(cg + 1) * 128, :], f"dp{pp}",
              after=[t_store[cg]], w=[B[f"PT{pp}"]])

    def emit_quant(cg):
        pp = cg % 2
        xb = B[f"XT{pp}"]
        P.op("v", lambda v, cg=cg: v.tensor_scalar_mul(
            S["SCOL"][:, 10:11], S["RSTD"][:, cg:cg + 1], S["INVSC"][:, 0:1]),
            r=rb("RSTD", "INVSC"), w=rb("SCOL"))
        P.op("v", lambda v, cg=cg: v.tensor_scalar(
            S["SCOL"][:, 11:12], S["MU"][:, cg:cg + 1], S["SCOL"][:, 10:11],
            -1.0, ALU.mult, ALU.mult), r=rb("MU", "SCOL"), w=rb("SCOL"))
        P.op("a", lambda a, pp=pp: a.activation(
            vQS, XT[pp][:, :], AF.Identity,
            bias=S["SCOL"][:, 11:12], scale=S["SCOL"][:, 10:11]),
            r=[xb] + rb("SCOL"), w=rb("QS"))
        P.op("v", lambda v: v.tensor_scalar(
            vQS, vQS, MAGIC, MAGIC, ALU.add, ALU.subtract),
            r=rb("QS"), w=rb("QS"))
        P.op("v", lambda v: v.tensor_scalar(
            vXQ, vQS, -128.0, 127.0, ALU.max, ALU.min),
            r=rb("QS"), w=rb("XQ"))
        for gk in range(2):
            for j in range(4):
                k = gk * 4 + j
                P.op("t", lambda t, k=k, j=j: t.transpose(
                    PBb[5][:, j * 128:(j + 1) * 128], vXQ[:, k * 128:(k + 1) * 128],
                    S["IDENTB"][:, :]), r=rb("XQ", "IDENTB"), w=rb("PB5"))
            P.op("a", lambda a, gk=gk: a.copy(
                vXQT[:, gk * 512:(gk + 1) * 512], PBb[5][:, 0:512]),
                r=rb("PB5"), w=rb("XQT"))

    emit_p2dma(0)
    if nchunks > 1:
        emit_p2dma(1)
    emit_quant(0)
    for cg in range(nchunks):
        pp = cg % 2
        for jh in range(2):
            for k in range(8):
                P.op("t", lambda t, jh=jh, k=k: t.matmul(
                    PB[jh][:, 0:512], vXQT[:, k * 128:(k + 1) * 128],
                    vWFT[:, k * 1024 + jh * 512:k * 1024 + (jh + 1) * 512],
                    start=(k == 0), stop=(k == 7)),
                    r=rb("XQT", "WFT"), w=[B[f"PB{jh}"]])
        if cg + 1 < nchunks:
            emit_quant(cg + 1)
        for jh in range(2):
            P.op("v", lambda v, jh=jh, pp=pp: v.scalar_tensor_tensor(
                OUT[pp][:, jh * 512:(jh + 1) * 512], PB[jh][:, 0:512],
                S["SCOLS"][:, 0:1], vFBIAS[:, jh * 512:(jh + 1) * 512],
                ALU.mult, ALU.add),
                r=[B[f"PB{jh}"], B["SCOLS"], B["FBIAS"]], w=[B[f"OUT{pp}"]])
            P.op("v", lambda v, jh=jh, pp=pp: v.scalar_tensor_tensor(
                OUT[pp][:, jh * 512:(jh + 1) * 512],
                vPT[pp][:, jh * 512:(jh + 1) * 512], 1.0,
                OUT[pp][:, jh * 512:(jh + 1) * 512], ALU.mult, ALU.add),
                r=[B[f"PT{pp}"], B[f"OUT{pp}"]], w=[B[f"OUT{pp}"]])
        P.dma("a", outd[cg * 128:(cg + 1) * 128, :], OUT[pp][:, :], f"do{pp}",
              r=[B[f"OUT{pp}"]])
        if cg + 2 < nchunks:
            emit_p2dma(cg + 2)

    P._do("sy", lambda E: E.nop(), [("do0", P.cnt["do0"]), ("do1", P.cnt["do1"])])

    with ctx:
        with nc.Block() as block:
            P.emit(block)
    return nc


_NC_CACHE = {}


def kernel(L0_attn_x, L0_attn_norm_w, L0_attn_norm_b, L0_attn_in_proj_w,
           L0_attn_conv_w, L0_attn_conv_b, L0_attn_dt_bias, L0_attn_A_log,
           L0_attn_D, L0_attn_rms_w, L0_attn_out_proj_w, L0_ffn_weight_int8,
           L0_ffn_weight_scale, L0_ffn_bias):
    x = np.ascontiguousarray(np.asarray(L0_attn_x, dtype=np.float32))
    nb = x.shape[0]
    shard = nb // NCORES
    common = {
        "norm_w": np.ascontiguousarray(L0_attn_norm_w, dtype=np.float32),
        "norm_b": np.ascontiguousarray(L0_attn_norm_b, dtype=np.float32),
        "in_proj_w": np.ascontiguousarray(L0_attn_in_proj_w, dtype=np.float32),
        "conv_w": np.ascontiguousarray(L0_attn_conv_w, dtype=np.float32),
        "conv_b": np.ascontiguousarray(L0_attn_conv_b, dtype=np.float32),
        "dt_bias": np.ascontiguousarray(L0_attn_dt_bias, dtype=np.float32),
        "A_log": np.ascontiguousarray(L0_attn_A_log, dtype=np.float32),
        "Dp": np.ascontiguousarray(L0_attn_D, dtype=np.float32),
        "rms_w": np.ascontiguousarray(L0_attn_rms_w, dtype=np.float32),
        "out_proj_w": np.ascontiguousarray(L0_attn_out_proj_w, dtype=np.float32),
        "ffn_w": np.ascontiguousarray(L0_ffn_weight_int8, dtype=np.int8),
        "ffn_scale": np.ascontiguousarray(L0_ffn_weight_scale, dtype=np.float32),
        "ffn_bias": np.ascontiguousarray(L0_ffn_bias, dtype=np.float32),
    }
    in_maps = []
    for c in range(NCORES):
        m = dict(common)
        m["x"] = np.ascontiguousarray(
            x[c * shard:(c + 1) * shard].reshape(TOKENS, D))
        in_maps.append(m)

    if "nc" not in _NC_CACHE:
        _NC_CACHE["nc"] = build_program()
    ncx = _NC_CACHE["nc"]
    res = run_bass_kernel_spmd(ncx, in_maps, core_ids=list(range(NCORES)))
    outs = [np.asarray(res.results[c]["out"]).reshape(shard, SEQLEN, D)
            for c in range(NCORES)]
    return np.concatenate(outs, axis=0)
